# version 15
# baseline (speedup 1.0000x reference)
import numpy as np
import ml_dtypes
import concourse.bass as bass
import concourse.mybir as mybir
from concourse.bass_utils import run_bass_kernel_spmd
from contextlib import ExitStack

F32 = mybir.dt.float32
BF16 = mybir.dt.bfloat16
AF = mybir.ActivationFunctionType
ALU = mybir.AluOpType
AX = mybir.AxisListType

T = 2048
DM = 1024
NH = 8
DH = 128
NE = 8
HF = 2048
KC = DM // 128
NTS = T // 512
NTT = T // 128
NEG = -1e30
EPS = 1e-5

ENGS = ("sync", "gpsimd", "tensor", "vector", "scalar")


class Ch:
    __slots__ = ("sem", "n")

    def __init__(self, sem):
        self.sem = sem
        self.n = 0


class Builder:

    def __init__(self):
        self.q = {e: [] for e in ENGS}
        self.lastw = {e: {} for e in ENGS}
        self.chain = {}
        self.prev_tok = {}

    def emit(self, eng, fn, waits=(), incs=(), nochain=False):
        lw = self.lastw[eng]
        incs = list(incs)
        cc = self.chain.get(eng)
        if nochain:
            cc = None
        if cc is not None:
            waits = list(waits) + ([self.prev_tok[eng]] if self.prev_tok.get(eng) else [])
            if not any(ch is cc for ch, _ in incs):
                incs.append((cc, 1))
        ws = []
        for tok in waits:
            if tok is None:
                continue
            ch, v = tok
            k = id(ch)
            if lw.get(k, 0) < v:
                ws.append((ch.sem, v))
                lw[k] = v
        toks = []
        rincs = []
        for ch, amt in incs:
            ch.n += amt
            toks.append((ch, ch.n))
            rincs.append((ch.sem, amt))
        if cc is not None:
            for ch, _ in incs:
                if ch is cc:
                    self.prev_tok[eng] = (cc, cc.n)

        def run(e, ws=ws, fn=fn, rincs=rincs):
            for sem, v in ws:
                e.wait_ge(sem, v)
            inst = fn(e)
            for sem, amt in rincs:
                inst = inst.then_inc(sem, amt)

        self.q[eng].append(run)
        return toks


class PsBank:
    __slots__ = ("ap", "ready", "frees")

    def __init__(self, ap):
        self.ap = ap
        self.ready = None
        self.frees = []


def host_consts():
    pos = np.arange(T, dtype=np.float64)
    dims = np.arange(64, dtype=np.float64)
    freq = 10000.0 ** (-dims / 64.0)
    rad = pos[None, :] * freq[:, None]
    cs = np.cos(rad)
    sn = np.sin(rad)
    cs_full = np.concatenate([cs, cs], 0).astype(np.float32)
    sn_signed = np.concatenate([-sn, sn], 0).astype(np.float32)
    r = np.arange(128)[:, None]
    u = np.arange(896)[None, :]
    maskM = np.where(u < r + 384, NEG, 0.0).astype(ml_dtypes.bfloat16)
    P = np.zeros((128, 128), dtype=np.float32)
    for rr in range(64):
        P[rr, rr + 64] = -1.0
        P[rr + 64, rr] = 1.0
    pmat = np.ascontiguousarray(P.T).astype(ml_dtypes.bfloat16)
    identb = np.eye(128, dtype=ml_dtypes.bfloat16)
    identf = np.eye(128, dtype=np.float32)
    return {"c_cs": cs_full, "c_sn": sn_signed, "c_mask": maskM,
            "c_identb": identb, "c_identf": identf, "c_pmat": pmat}


def build_nc(dbg=False):
    nc = bass.Bass("TRN2", target_bir_lowering=False, debug=False)

    d = {}
    d["x"] = nc.dram_tensor("x", [T, DM], F32, kind="ExternalInput").ap()
    d["wq"] = nc.dram_tensor("wq", [DM, DM], F32, kind="ExternalInput").ap()
    d["wk"] = nc.dram_tensor("wk", [DM, DM], F32, kind="ExternalInput").ap()
    d["wv"] = nc.dram_tensor("wv", [DM, DM], F32, kind="ExternalInput").ap()
    d["wo"] = nc.dram_tensor("wo", [DM, DM], F32, kind="ExternalInput").ap()
    d["rw"] = nc.dram_tensor("rw", [NE, DM], F32, kind="ExternalInput").ap()
    d["eb"] = nc.dram_tensor("eb", [NE], F32, kind="ExternalInput").ap()
    d["w1"] = nc.dram_tensor("w1", [NE + 1, DM, HF], F32, kind="ExternalInput").ap()
    d["w3"] = nc.dram_tensor("w3", [NE + 1, DM, HF], F32, kind="ExternalInput").ap()
    d["w2"] = nc.dram_tensor("w2", [NE + 1, HF, DM], F32, kind="ExternalInput").ap()
    d["c_cs"] = nc.dram_tensor("c_cs", [128, T], F32, kind="ExternalInput").ap()
    d["c_sn"] = nc.dram_tensor("c_sn", [128, T], F32, kind="ExternalInput").ap()
    d["c_mask"] = nc.dram_tensor("c_mask", [128, 896], BF16, kind="ExternalInput").ap()
    d["c_identb"] = nc.dram_tensor("c_identb", [128, 128], BF16, kind="ExternalInput").ap()
    d["c_identf"] = nc.dram_tensor("c_identf", [128, 128], F32, kind="ExternalInput").ap()
    d["c_pmat"] = nc.dram_tensor("c_pmat", [128, 128], BF16, kind="ExternalInput").ap()
    d["out"] = nc.dram_tensor("out", [T, DM], F32, kind="ExternalOutput").ap()
    if dbg:
        dbg = set(dbg) if not isinstance(dbg, bool) else {"xn1","xattn","o","v","cw","xn2","qk"}
        d["d_xn1"] = nc.dram_tensor("d_xn1", [128, KC * T], BF16, kind="ExternalOutput").ap()
        d["d_xattn"] = nc.dram_tensor("d_xattn", [128, KC * T], F32, kind="ExternalOutput").ap()
        d["d_o"] = nc.dram_tensor("d_o", [128, 16384], BF16, kind="ExternalOutput").ap()
        d["d_v"] = nc.dram_tensor("d_v", [128, 8192], BF16, kind="ExternalOutput").ap()
        d["d_cw"] = nc.dram_tensor("d_cw", [128, 128], F32, kind="ExternalOutput").ap()
        d["d_xn2"] = nc.dram_tensor("d_xn2", [128, KC * T], BF16, kind="ExternalOutput").ap()
        d["d_qk"] = nc.dram_tensor("d_qk", [128, 8192], BF16, kind="ExternalOutput").ap()
        d["d_lg"] = nc.dram_tensor("d_lg", [128, 128], F32, kind="ExternalOutput").ap()
        d["d_km"] = nc.dram_tensor("d_km", [128, 48], F32, kind="ExternalOutput").ap()

    x_sb = nc.alloc_sbuf_tensor("x_sb", [128, KC * T], F32)
    xn_sb = nc.alloc_sbuf_tensor("xn_sb", [128, KC * T], BF16)
    big = nc.alloc_sbuf_tensor("big", [128, 32768], BF16)
    w_sb = nc.alloc_sbuf_tensor("w_sb", [128, 8192], BF16)
    csn = nc.alloc_sbuf_tensor("csn", [128, 4096], BF16)
    s_sb = nc.alloc_sbuf_tensor("s_sb", [128, 3072], F32)
    maskm = nc.alloc_sbuf_tensor("maskm", [128, 896], BF16)
    pT = nc.alloc_sbuf_tensor("pT", [128, 1024], BF16)
    identb = nc.alloc_sbuf_tensor("identb", [8, 16], BF16)
    identf = nc.alloc_sbuf_tensor("identf", [128, 128], F32)
    pmat_sb = nc.alloc_sbuf_tensor("pmat_sb", [128, 128], BF16)
    onesb = nc.alloc_sbuf_tensor("onesb", [128, 1], BF16)
    ones1 = nc.alloc_sbuf_tensor("ones1", [1, 128], F32)
    ones1b = nc.alloc_sbuf_tensor("ones1b", [1, 128], BF16)
    rwT = nc.alloc_sbuf_tensor("rwT", [128, KC * 8], BF16)
    rw_raw = nc.alloc_sbuf_tensor("rw_raw", [8, DM], BF16)
    bias_rep = nc.alloc_sbuf_tensor("bias_rep", [128, 8], F32)
    cwtok = nc.alloc_sbuf_tensor("cwtok", [128, 128], F32)
    lg_sb = nc.alloc_sbuf_tensor("lg_sb", [128, 128], F32)
    m_acc = nc.alloc_sbuf_tensor("m_acc", [128, 1], F32)
    m_all = nc.alloc_sbuf_tensor("m_all", [128, 40], F32)
    ta64 = nc.alloc_sbuf_tensor("ta64", [128, 64], F32)
    m1a = nc.alloc_sbuf_tensor("m1a", [128, 16], F32)
    m2a = nc.alloc_sbuf_tensor("m2a", [128, 16], F32)
    kd128 = nc.alloc_sbuf_tensor("kd128", [128, 128], F32)
    km1a = nc.alloc_sbuf_tensor("km1a", [128, 128], F32)
    km2a = nc.alloc_sbuf_tensor("km2a", [128, 128], F32)
    mska = nc.alloc_sbuf_tensor("mska", [128, 128], F32)
    zer128 = nc.alloc_sbuf_tensor("zer128", [128, 128], F32)
    d12a = nc.alloc_sbuf_tensor("d12a", [128, 16], F32)
    p1a = nc.alloc_sbuf_tensor("p1a", [128, 16], F32)
    p2a = nc.alloc_sbuf_tensor("p2a", [128, 16], F32)
    m_tmp = nc.alloc_sbuf_tensor("m_tmp", [128, 1], F32)
    negmh = nc.alloc_sbuf_tensor("negmh", [128, 1], F32)
    mh_sb = nc.alloc_sbuf_tensor("mh_sb", [1, 8], F32)

    bigf = big[:, :].bitcast(F32)
    csnf = csn[:, :].bitcast(F32)
    sf = s_sb

    ps = [PsBank(nc.alloc_psum_tensor(f"ps{i}", [128, 512], F32).ap()) for i in range(8)]

    stack = ExitStack()
    mk = lambda name: Ch(stack.enter_context(nc.semaphore(name)))
    ld = mk("ldc")
    ldg = mk("ldg")
    ld_stage = [mk("lds0"), mk("lds1")]
    sw_rope = [mk("swr0"), mk("swr1")]
    sw_cw = mk("swcw")
    st_out = [mk("sto0"), mk("sto1")]
    st_dbg = mk("stdbg")
    wd_qk = [mk("wqk0"), mk("wqk1")]
    wd_wv = mk("wwv")
    wd_wo = [mk("wwo0"), mk("wwo1")]
    wd_w1 = [mk("ww10"), mk("ww11")]
    wd_w3 = [mk("ww30"), mk("ww31")]
    wd_w2 = [mk("ww20"), mk("ww21")]
    pe = mk("pe")
    dv = mk("dv")
    ac = mk("ac")
    gp = mk("gp")
    st = st_dbg
    nc._ch_stack = stack

    b = Builder()
    b.chain["vector"] = dv
    b.chain["scalar"] = ac
    b.chain["gpsimd"] = gp

    def dma(ch, eng, out_ap, in_ap, waits=()):
        return b.emit(eng, lambda e, o=out_ap, i=in_ap: e.dma_start(out=o, in_=i),
                      waits=waits, incs=[(ch, 16)], nochain=True)[0]

    def bank_prod(bank, fn, waits=(), last=True):
        w = list(waits) + bank.frees
        toks = b.emit("tensor", fn, waits=w, incs=[(pe, 1)] if last else ())
        bank.frees = []
        if last:
            bank.ready = toks[0]
            return toks[0]
        return None

    def bank_cons(bank, eng, fn, waits=(), ch=dv):
        w = list(waits) + ([bank.ready] if bank.ready else [])
        tok = b.emit(eng, fn, waits=w, incs=[(ch, 1)])[0]
        bank.frees.append(tok)
        return tok

    t_identb = dma(ld, "sync", identb[0:8, 0:16], d["c_identb"][0:8, 0:16])
    t_identf = dma(ld, "sync", identf[:, :], d["c_identf"])
    t_mask = dma(ld, "sync", maskm[:, :], d["c_mask"])
    t_eb = dma(ld, "sync", sf[0:1, 0:8], d["eb"].rearrange("(a b) -> a b", a=1))
    t_pmat = dma(ld, "sync", pmat_sb[:, :], d["c_pmat"])
    t_rw = dma(ldg, "gpsimd", rw_raw[:, :], d["rw"])
    t_cs = dma(ldg, "gpsimd", csn[:, 0:T], d["c_cs"])
    t_sn = dma(ldg, "gpsimd", csn[:, T:2 * T], d["c_sn"])

    t_consts = (ld, ld.n)
    t_constsg = (ldg, ldg.n)
    t_identb = t_identf = t_mask = t_eb = t_pmat = t_consts
    t_rw = t_cs = t_sn = t_constsg
    b.emit("vector", lambda e: e.memset(onesb[:, :], 1.0), incs=[(dv, 1)])
    b.emit("vector", lambda e: e.memset(zer128[:, :], 0.0), incs=[(dv, 1)])
    t_ones1 = b.emit("vector", lambda e: e.memset(ones1[:, :], 1.0), incs=[(dv, 1)])[0]
    b.emit("vector", lambda e: e.memset(ones1b[:, :], 1.0), incs=[(dv, 1)])

    WQ0, WK0, WVG = 0, 2048, 4096
    wq_flat = d["wq"].rearrange("(kc p) n -> p kc n", p=128)
    wk_flat = d["wk"].rearrange("(kc p) n -> p kc n", p=128)
    wv_flat = d["wv"].rearrange("(kc p) n -> p kc n", p=128)
    wo_flat = d["wo"].rearrange("(c p) m -> p c m", p=128)

    def wq_slice(par):
        return w_sb[:, WQ0 + par * 1024: WQ0 + (par + 1) * 1024]

    def wk_slice(par):
        return w_sb[:, WK0 + par * 1024: WK0 + (par + 1) * 1024]

    wv_group = w_sb[:, WVG:WVG + 4096]

    t_wq = [None] * NH
    t_wk = [None] * NH
    t_wv = [None, None]

    def prefetch_qk(h, waits=()):
        par = h % 2
        dq = wq_slice(par).rearrange("p (kc n) -> p kc n", kc=KC)
        dma(wd_qk[par], "gpsimd", dq, wq_flat[:, :, h * DH:(h + 1) * DH], waits=waits)
        dk = wk_slice(par).rearrange("p (kc n) -> p kc n", kc=KC)
        tok = dma(wd_qk[par], "gpsimd", dk, wk_flat[:, :, h * DH:(h + 1) * DH])
        t_wq[h] = t_wk[h] = tok

    def prefetch_wv(half, waits=()):
        dvv = wv_group.rearrange("p (kc n) -> p kc n", kc=KC)
        t_wv[half] = dma(wd_wv, "gpsimd", dvv, wv_flat[:, :, half * 512:(half + 1) * 512],
                         waits=waits)

    prefetch_wv(0)
    prefetch_qk(0)
    prefetch_qk(1)

    x_tiles = d["x"].rearrange("(g p) m -> p g m", p=128)
    stage = [bigf[:, 0:4096], bigf[:, 4096:8192]]
    stage_free = [None, None]

    slot = 0
    for g4 in range(4):
        par = g4 % 2
        wgrp = [stage_free[par]] if stage_free[par] else []
        tok_ld = dma(ld_stage[par], "sync",
                     stage[par].rearrange("p (g m) -> p g m", g=4),
                     x_tiles[:, g4 * 4:(g4 + 1) * 4, :], waits=wgrp)
        for ti in range(4):
            tt = g4 * 4 + ti
            for kc in range(KC):
                src = stage[par][:, ti * 1024 + kc * 128: ti * 1024 + (kc + 1) * 128]
                bank = ps[slot % 2]
                sl = (slot // 2) % 4
                pap = bank.ap[:, sl * 128:(sl + 1) * 128]
                bank_prod(bank, lambda e, o=pap, i=src: e.transpose(o, i, identf[:, :]),
                          waits=[tok_ld, t_identf])
                dstc = kc * T + tt * 128
                bank_cons(bank, "vector",
                          lambda e, o=x_sb[:, dstc:dstc + 128], i=pap: e.tensor_copy(o, i))
                slot += 1
        stage_free[par] = (pe, pe.n)
    t_x_ready = (dv, dv.n)

    for kc in range(KC):
        bank = ps[2]
        pap = bank.ap[:, 0:4].bitcast(BF16)
        bank_prod(bank, lambda e, o=pap, i=rw_raw[:, kc * 128:(kc + 1) * 128]:
                  e.transpose(o, i, identb[0:8, 0:8]),
                  waits=[t_rw, t_identb])
        bank_cons(bank, "vector",
                  lambda e, o=rwT[:, kc * 8:(kc + 1) * 8], i=pap: e.tensor_copy(o, i))
    bank = ps[3]
    pap = bank.ap[:, 0:8]
    bank_prod(bank, lambda e, o=pap: e.matmul(o, ones1[:, :], sf[0:1, 0:8],
                                              start=True, stop=True),
              waits=[t_eb, t_ones1])
    bank_cons(bank, "vector",
              lambda e, o=bias_rep[:, :], i=pap: e.tensor_copy(o, i))

    def layernorm(t_x_tok):
        if not isinstance(t_x_tok, list):
            t_x_tok = [t_x_tok]
        xn_tok = []
        U0 = sf[0:1, 2048:2560]
        U1 = sf[0:1, 2560:3072]
        U2 = sf[0:1, 1536:2048]
        a_rep = sf[:, 0:512]
        b_rep = sf[:, 512:1024]
        tmp = sf[:, 1024:1536]
        t_gp_prev = None
        for ts in range(NTS):
            tcol = ts * 512
            t_cp = None
            for kc in range(KC):
                col = kc * T + tcol
                t_cp = b.emit("vector",
                              lambda e, o=xn_sb[:, col:col + 512],
                              i=x_sb[:, col:col + 512]: e.tensor_copy(o, i),
                              waits=t_x_tok, incs=[(dv, 1)])[0]
            bank_s = ps[0]
            for kc in range(KC):
                col = kc * T + tcol
                bank_prod(bank_s,
                          lambda e, r=xn_sb[:, col:col + 512],
                          s0=(kc == 0), s1=(kc == KC - 1):
                          e.matmul(bank_s.ap[0:1, :], onesb[:, :], r, start=s0, stop=s1),
                          waits=[t_cp], last=(kc == KC - 1))
            t_sum = bank_s.ready
            xsq = [sf[:, 2560 + i * 256: 2560 + (i + 1) * 256].bitcast(BF16)
                   for i in range(2)]
            bank_q = ps[1]
            t_xsq_pe = [None, None]
            for kc in range(KC):
                xa = xsq[kc % 2]
                col = kc * T + tcol
                w = [t_cp, t_xsq_pe[kc % 2]]
                t_sq = b.emit("vector",
                              lambda e, o=xa, i=xn_sb[:, col:col + 512]: e.tensor_mul(o, i, i),
                              waits=w, incs=[(dv, 1)])[0]
                tokp = bank_prod(bank_q,
                                 lambda e, r=xa, s0=(kc == 0), s1=(kc == KC - 1):
                                 e.matmul(bank_q.ap[0:1, :], onesb[:, :], r, start=s0, stop=s1),
                                 waits=[t_sq], last=True)
                t_xsq_pe[kc % 2] = tokp
            t_ssq = bank_q.ready
            t1 = b.emit("vector", lambda e: e.tensor_scalar_mul(U0, bank_s.ap[0:1, :], 1.0 / DM),
                        waits=[t_sum], incs=[(dv, 1)])[0]
            bank_s.frees.append(t1)
            t2 = b.emit("vector", lambda e: e.tensor_scalar_mul(U1, bank_q.ap[0:1, :], 1.0 / DM),
                        waits=[t_ssq], incs=[(dv, 1)])[0]
            bank_q.frees.append(t2)
            b.emit("vector", lambda e: e.tensor_mul(U2, U0, U0), incs=[(dv, 1)])
            b.emit("vector", lambda e: e.tensor_sub(U1, U1, U2), incs=[(dv, 1)])
            t3 = b.emit("vector", lambda e: e.tensor_scalar_add(U1, U1, EPS),
                        incs=[(dv, 1)])[0]
            t4 = b.emit("scalar", lambda e: e.activation(U2, U1, AF.Ln),
                        waits=[t3], incs=[(ac, 1)])[0]
            t5 = b.emit("scalar", lambda e: e.activation(U1, U2, AF.Exp, scale=-0.5),
                        waits=[t4], incs=[(ac, 1)])[0]
            t6 = b.emit("vector", lambda e: e.tensor_scalar_mul(U2, U0, -1.0),
                        waits=[t5], incs=[(dv, 1)])[0]
            t7 = b.emit("vector", lambda e: e.tensor_mul(U0, U2, U1), incs=[(dv, 1)])[0]
            bank_a, bank_b = ps[2], ps[3]
            bank_prod(bank_a, lambda e: e.matmul(bank_a.ap[:, :], ones1[:, :], U1,
                                                 start=True, stop=True),
                      waits=[t5, t6, t_ones1])
            bank_prod(bank_b, lambda e: e.matmul(bank_b.ap[:, :], ones1[:, :], U0,
                                                 start=True, stop=True),
                      waits=[t7])
            ta = bank_cons(bank_a, "vector", lambda e: e.tensor_copy(a_rep, bank_a.ap[:, :]),
                           waits=[t_gp_prev])
            tb = bank_cons(bank_b, "vector", lambda e: e.tensor_copy(b_rep, bank_b.ap[:, :]))
            t_last = None
            for kc in range(KC):
                col = kc * T + tcol
                tm = b.emit("gpsimd",
                            lambda e, i=x_sb[:, col:col + 512]: e.tensor_mul(tmp, i, a_rep),
                            waits=[ta, t_sum, t_ssq, t_last], incs=[(gp, 1)])[0]
                t_last = b.emit("gpsimd",
                                lambda e, o=xn_sb[:, col:col + 512]: e.tensor_add(o, tmp, b_rep),
                                waits=[tb, tm], incs=[(gp, 1)])[0]
            t_gp_prev = t_last
            xn_tok.append(t_last)
        return xn_tok

    dbg_x0_tok = []
    if dbg and "x0" in dbg:
        dbg_x0_tok.append(dma(st, "sync", d["d_xattn"], x_sb[:, :], waits=[t_x_ready]))
    xn_tok = layernorm(t_x_ready)
    dbg_gate_x2 = []
    if dbg and "xn1" in dbg:
        dbg_gate_x2.append(dma(st, "sync", d["d_xn1"], xn_sb[:, :], waits=xn_tok))

    QB = 24576
    o_col = 8192

    def q_slice(par):
        return big[:, QB + par * 4096: QB + par * 4096 + 2048]

    def k_slice(par):
        return big[:, QB + par * 4096 + 2048: QB + par * 4096 + 4096]

    sbf = s_sb[:, :].bitcast(BF16)
    rope_qb = [sbf[:, i * 512:(i + 1) * 512] for i in range(2)]
    rope_t1 = [sf[:, 1024 + i * 512: 1024 + (i + 1) * 512] for i in range(2)]
    scale_rep = sf[:, 2048:2560]
    invl_row = sf[0:1, 2560:3072]
    mh_row = mh_sb[0:1, 0:8]
    qb_free_tok = [None] * 2
    t_rot_prev = [None]

    qk_free = [None, None]
    t_v_done = [None, None]

    def emit_v(half):
        for tt in range(NTT):
            bank = ps[tt % 2]
            for kc in range(KC):
                lhs = xn_sb[:, kc * T + tt * 128: kc * T + (tt + 1) * 128]
                bank_prod(bank,
                          lambda e, l=lhs, r=wv_group[:, kc * 512:(kc + 1) * 512],
                          o=bank.ap[:, :], s0=(kc == 0), s1=(kc == KC - 1):
                          e.matmul(o, l, r, start=s0, stop=s1),
                          waits=[t_wv[half]] + xn_tok, last=(kc == KC - 1))
            bank_cons(bank, "vector",
                      lambda e, o=big[:, tt * 512:(tt + 1) * 512], i=bank.ap[:, :]:
                      e.tensor_copy(o, i))
        t_v_done[half] = (dv, dv.n)
        if half == 0:
            prefetch_wv(1, waits=[(pe, pe.n)])

    def emit_qkv_rope(h):
        par = h % 2
        for which in range(2):
            wslice = wq_slice(par) if which == 0 else wk_slice(par)
            t_w = t_wq[h]
            dst = q_slice(par) if which == 0 else k_slice(par)
            for kc in range(KC):
                for ts in range(NTS):
                    bank = ps[ts]
                    col = kc * T + ts * 512
                    bank_prod(bank,
                              lambda e, o=bank.ap[:, :],
                              l=wslice[:, kc * 128:(kc + 1) * 128],
                              r=xn_sb[:, col:col + 512],
                              s0=(kc == 0), s1=(kc == KC - 1):
                              e.matmul(o, l, r, start=s0, stop=s1),
                              waits=[t_w, xn_tok[ts], qk_free[par]],
                              last=(kc == KC - 1))
            for ts in range(NTS):
                sl = ts % 2
                bank = ps[ts]
                pqb = ps[4 + sl]
                qb = rope_qb[sl]
                tcs = csn[:, ts * 512:(ts + 1) * 512]
                tsn = csn[:, T + ts * 512: T + ts * 512 + 512]
                te = bank_cons(bank, "vector",
                               lambda e, o=qb, i=bank.ap[:, :]: e.tensor_copy(o, i),
                               waits=[qb_free_tok[sl]])
                qb_free_tok[sl] = bank_prod(pqb,
                                            lambda e, o=pqb.ap[:, :], r=qb:
                                            e.matmul(o, pmat_sb[:, :], r,
                                                     start=True, stop=True),
                                            waits=[te, t_pmat])
                t_t1a = bank_cons(bank, "vector",
                          lambda e, o=rope_t1[0], i=bank.ap[:, :], c=tcs:
                          e.tensor_mul(o, i, c),
                          waits=[t_cs, t_rot_prev[0]])
                t_t1b = bank_cons(pqb, "vector",
                          lambda e, o=rope_t1[1], i=pqb.ap[:, :], c=tsn:
                          e.tensor_mul(o, i, c),
                          waits=[t_sn, t_rot_prev[0]])
                t_rot_prev[0] = b.emit("gpsimd",
                       lambda e, o=dst[:, ts * 512:(ts + 1) * 512]:
                       e.tensor_add(o, rope_t1[0], rope_t1[1]),
                       waits=[t_t1a, t_t1b], incs=[(gp, 1)])[0]
        return [(dv, dv.n), (gp, gp.n)]

    rope_done = [None] * NH
    pT_free = [None]

    def emit_passA(h):
        par = h % 2
        qs = q_slice(par)
        ks = k_slice(par)
        sq_scr = [pT[:, 0:512], pT[:, 512:1024]]
        nb = [ps[6], ps[7]]
        for which in range(2):
            src = qs if which == 0 else ks
            bank = nb[which]
            t_prev_mm = pT_free[0]
            for ts in range(NTS):
                tsq = b.emit("gpsimd",
                             lambda e, o=sq_scr[which],
                             i=src[:, ts * 512:(ts + 1) * 512]: e.tensor_mul(o, i, i),
                             waits=rope_done[h] + [t_prev_mm], incs=[(gp, 1)])[0]
                t_prev_mm = bank_prod(bank,
                                      lambda e, o=bank.ap[0:1, :], r=sq_scr[which],
                                      s0=(ts == 0), s1=(ts == NTS - 1):
                                      e.matmul(o, onesb[:, :], r, start=s0, stop=s1),
                                      waits=[tsq], last=True)
            bank_cons(bank, "vector",
                      lambda e, o=mh_row[0:1, which:which + 1], i=bank.ap[0:1, :]:
                      e.reduce_max(o, i, axis=AX.X))
        b.emit("vector",
               lambda e: e.tensor_mul(mh_row[0:1, 2:3], mh_row[0:1, 0:1],
                                      mh_row[0:1, 1:2]),
               incs=[(dv, 1)])
        t_mm2 = b.emit("vector",
               lambda e: e.tensor_scalar_add(mh_row[0:1, 2:3], mh_row[0:1, 2:3], 1e-30),
               incs=[(dv, 1)])[0]
        t_ln = b.emit("scalar",
                      lambda e: e.activation(mh_row[0:1, 3:4], mh_row[0:1, 2:3], AF.Ln),
                      waits=[t_mm2], incs=[(ac, 1)])[0]
        t_bd = b.emit("scalar",
                      lambda e: e.activation(mh_row[0:1, 2:3], mh_row[0:1, 3:4], AF.Exp,
                                             scale=0.5),
                      incs=[(ac, 1)])[0]
        tneg = b.emit("vector",
                      lambda e: e.tensor_scalar_mul(mh_row[0:1, 3:4], mh_row[0:1, 2:3],
                                                    -1.02 / DH),
                      waits=[t_bd], incs=[(dv, 1)])[0]
        bank = ps[6]
        bank_prod(bank, lambda e, o=bank.ap[:, 0:1]:
                  e.matmul(o, ones1[:, :], mh_row[0:1, 3:4], start=True, stop=True),
                  waits=[tneg])
        return bank_cons(bank, "vector",
                         lambda e, o=negmh[:, :], i=bank.ap[:, 0:1]: e.tensor_copy(o, i),
                         waits=[(ac, ac.n)])

    def emit_passB(h, t_negmh):
        par = h % 2
        half = h // 4
        hh = h % 4
        qs = q_slice(par)
        ks = k_slice(par)
        for qc in range(NTS):
            ob = ps[6]
            lb = ps[7]
            nkt = 4 * qc + 4
            pend = None

            def emit_ol(item, s1):
                tex_, vsl_, ptb_, kt_ = item
                bank_prod(ob,
                          lambda e, o=ob.ap[:, :], l=vsl_, r=ptb_,
                          s0=(kt_ == 0), s1_=s1:
                          e.matmul(o, l, r, start=s0, stop=s1_),
                          waits=[tex_, t_v_done[half]], last=s1)
                bank_prod(lb,
                          lambda e, o=lb.ap[0:1, :], r=ptb_,
                          s0=(kt_ == 0), s1_=s1:
                          e.matmul(o, onesb[:, :], r, start=s0, stop=s1_),
                          waits=[tex_], last=s1)

            for kt in range(nkt):
                sb_ = ps[4 + (kt % 2)]
                bank_prod(sb_,
                          lambda e, o=sb_.ap[:, :],
                          l=ks[:, kt * 128:(kt + 1) * 128],
                          r=qs[:, qc * 512:(qc + 1) * 512]:
                          e.matmul(o, l, r, start=True, stop=True),
                          waits=rope_done[h])
                if kt >= 4 * qc:
                    moff = 384 - 128 * (kt - 4 * qc)
                    tmk = bank_cons(sb_, "vector",
                                    lambda e, o=sb_.ap[:, :],
                                    m=maskm[:, moff:moff + 512]: e.tensor_add(o, o, m),
                                    waits=[t_mask])
                    sb_.ready = tmk
                ptb = pT[:, (kt % 2) * 512: (kt % 2) * 512 + 512]
                tex = bank_cons(sb_, "scalar",
                                lambda e, o=ptb, i=sb_.ap[:, :]:
                                e.activation(o, i, AF.Exp, bias=negmh[:, 0:1],
                                             scale=1.0 / DH),
                                waits=[t_negmh], ch=ac)
                vsl = big[:, kt * 512 + hh * 128: kt * 512 + hh * 128 + 128]
                if pend is not None:
                    emit_ol(pend, s1=False)
                pend = (tex, vsl, ptb, kt)
            emit_ol(pend, s1=True)
            til = bank_cons(lb, "vector",
                            lambda e, o=invl_row, i=lb.ap[0:1, :]: e.reciprocal(o, i))
            bank5 = ps[5]
            bank_prod(bank5,
                      lambda e, o=bank5.ap[:, :]:
                      e.matmul(o, ones1[:, :], invl_row, start=True, stop=True),
                      waits=[til])
            tsc = bank_cons(bank5, "vector",
                            lambda e, o=scale_rep, i=bank5.ap[:, :]: e.tensor_copy(o, i))
            ocol = o_col + h * T + qc * 512
            bank_cons(ob, "vector",
                      lambda e, o=big[:, ocol:ocol + 512], i=ob.ap[:, :]:
                      e.tensor_mul(o, i, scale_rep),
                      waits=[tsc])
        qk_free[par] = (pe, pe.n)
        pT_free[0] = (pe, pe.n)

    emit_v(0)
    rope_done[0] = emit_qkv_rope(0)
    for h in range(NH):
        t_negmh = emit_passA(h)
        if h + 1 < NH:
            if h + 1 == 4:
                emit_v(1)
            rope_done[h + 1] = emit_qkv_rope(h + 1)
            if h + 2 < NH:
                prefetch_qk(h + 2, waits=[(pe, pe.n)])
        emit_passB(h, t_negmh)

    t_o_all = (dv, dv.n)

    def wo_slice(par):
        return w_sb[:, par * 1024: (par + 1) * 1024]

    t_wo_mc = [None] * KC
    for mc in range(2):
        dwo = wo_slice(mc % 2).rearrange("p (c n) -> p c n", c=KC)
        t_wo_mc[mc] = dma(wd_wo[mc % 2], "gpsimd", dwo, wo_flat[:, :, mc * 128:(mc + 1) * 128],
                          waits=[qk_free[mc % 2]])
    for mc in range(KC):
        par = mc % 2
        for ts in range(NTS):
            bank = ps[(mc * NTS + ts) % 4]
            for c in range(KC):
                ocol2 = o_col + c * T + ts * 512
                bank_prod(bank,
                          lambda e, o=bank.ap[:, :],
                          l=wo_slice(par)[:, c * 128:(c + 1) * 128],
                          r=big[:, ocol2:ocol2 + 512],
                          s0=(c == 0), s1=(c == KC - 1):
                          e.matmul(o, l, r, start=s0, stop=s1),
                          waits=[t_wo_mc[mc], t_o_all], last=(c == KC - 1))
            xcol = mc * T + ts * 512
            bank_cons(bank, "vector",
                      lambda e, o=x_sb[:, xcol:xcol + 512], i=bank.ap[:, :]:
                      e.tensor_add(o, o, i),
                      waits=dbg_x0_tok)
        if mc + 2 < KC:
            dwo = wo_slice(par).rearrange("p (c n) -> p c n", c=KC)
            t_wo_mc[mc + 2] = dma(wd_wo[par], "gpsimd", dwo,
                                  wo_flat[:, :, (mc + 2) * 128:(mc + 3) * 128],
                                  waits=[(pe, pe.n)])
    t_x2 = (dv, dv.n)
    if dbg and "o" in dbg:
        dbg_gate_x2.append(dma(st, "sync", d["d_o"], big[:, 8192:24576],
                               waits=[t_o_all]))
    if dbg and "xattn" in dbg:
        dbg_gate_x2.append(dma(st, "sync", d["d_xattn"], x_sb[:, :], waits=[t_x2]))

    xn_tok = layernorm([t_x2] + dbg_gate_x2)

    t_cw_rows = []
    t_lg = []
    for tt in range(NTT):
        bank = ps[4 + tt % 2]
        for kc in range(KC):
            bank_prod(bank,
                      lambda e, o=bank.ap[:, 0:8],
                      l=xn_sb[:, kc * T + tt * 128: kc * T + (tt + 1) * 128],
                      r=rwT[:, kc * 8:(kc + 1) * 8],
                      s0=(kc == 0), s1=(kc == KC - 1):
                      e.matmul(o, l, r, start=s0, stop=s1),
                      waits=[xn_tok[tt // 4]], last=(kc == KC - 1))
        t_lg.append(bank_cons(bank, "vector",
                              lambda e, i=bank.ap[:, 0:8],
                              o=lg_sb[:, tt * 8:(tt + 1) * 8]:
                              e.tensor_add(o, i, bias_rep[:, :])))
    lg3 = lg_sb[:, :].rearrange("p (g e) -> p g e", e=8)
    ta3 = ta64[:, :].rearrange("p (g e) -> p g e", e=4)

    def vnop():
        pass

    def m1bc(t):
        return t[:, :].rearrange("p (g o) -> p g o", o=1).broadcast_to([128, 16, 8])

    b.emit("vector", lambda e: e.tensor_max(ta3, lg3[:, :, 0:4], lg3[:, :, 4:8]),
           waits=t_lg, incs=[(dv, 1)])
    vnop()
    b.emit("vector", lambda e: e.tensor_max(kd128[:, 0:32].rearrange("p (g e) -> p g e", e=2),
                                            ta3[:, :, 0:2], ta3[:, :, 2:4]),
           incs=[(dv, 1)])
    vnop()
    b.emit("vector", lambda e: e.tensor_max(m1a[:, :], kd128[:, 0:32:2], kd128[:, 1:32:2]),
           incs=[(dv, 1)])
    vnop()
    b.emit("vector", lambda e: e.tensor_sub(kd128[:, :], lg_sb[:, :], m1bc(m1a)),
           incs=[(dv, 1)])
    vnop()
    b.emit("vector", lambda e: e.tensor_tensor(km1a[:, :], kd128[:, :], zer128[:, :],
                                               ALU.is_equal),
           incs=[(dv, 1)])
    vnop()
    b.emit("vector", lambda e: e.tensor_scalar_mul(mska[:, :], km1a[:, :], -3e38),
           incs=[(dv, 1)])
    vnop()
    b.emit("vector", lambda e: e.tensor_add(mska[:, :], lg_sb[:, :], mska[:, :]),
           incs=[(dv, 1)])
    vnop()
    msk3 = mska[:, :].rearrange("p (g e) -> p g e", e=8)
    b.emit("vector", lambda e: e.tensor_max(ta3, msk3[:, :, 0:4], msk3[:, :, 4:8]),
           incs=[(dv, 1)])
    vnop()
    b.emit("vector", lambda e: e.tensor_max(kd128[:, 0:32].rearrange("p (g e) -> p g e", e=2),
                                            ta3[:, :, 0:2], ta3[:, :, 2:4]),
           incs=[(dv, 1)])
    vnop()
    b.emit("vector", lambda e: e.tensor_max(m2a[:, :], kd128[:, 0:32:2], kd128[:, 1:32:2]),
           incs=[(dv, 1)])
    vnop()
    b.emit("vector", lambda e: e.tensor_sub(d12a[:, :], m1a[:, :], m2a[:, :]),
           incs=[(dv, 1)])
    vnop()
    b.emit("vector", lambda e: e.tensor_sub(kd128[:, :], lg_sb[:, :], m1bc(m2a)),
           incs=[(dv, 1)])
    vnop()
    t_d = b.emit("vector", lambda e: e.tensor_tensor(km2a[:, :], kd128[:, :], zer128[:, :],
                                                     ALU.is_equal),
                 incs=[(dv, 1)])[0]
    tp1 = b.emit("scalar", lambda e: e.activation(p1a[:, :], d12a[:, :], AF.Sigmoid),
                 waits=[t_d], incs=[(ac, 1)])[0]
    tp2 = b.emit("scalar", lambda e: e.activation(p2a[:, :], d12a[:, :], AF.Sigmoid,
                                                  scale=-1.0),
                 incs=[(ac, 1)])[0]
    b.emit("vector", lambda e: e.tensor_mul(mska[:, :].rearrange("p (g e) -> p g e", e=8),
                                            km1a[:, :].rearrange("p (g e) -> p g e", e=8),
                                            m1bc(p1a)),
           waits=[tp1], incs=[(dv, 1)])
    b.emit("vector", lambda e: e.tensor_mul(kd128[:, :].rearrange("p (g e) -> p g e", e=8),
                                            km2a[:, :].rearrange("p (g e) -> p g e", e=8),
                                            m1bc(p2a)),
           waits=[tp2], incs=[(dv, 1)])
    vnop()
    tcw = b.emit("vector", lambda e: e.tensor_add(cwtok[:, :], mska[:, :], kd128[:, :]),
                 incs=[(dv, 1)])[0]
    for tt in range(NTT):
        bankt = ps[6]
        papt = bankt.ap[0:8, 0:128]
        bank_prod(bankt,
                  lambda e, o=papt, i=cwtok[:, tt * 8:(tt + 1) * 8]:
                  e.transpose(o, i, identf[:, :]),
                  waits=[tcw])
        t_cw_rows.append(
            bank_cons(bankt, "vector",
                      lambda e, o=csn[0:8, tt * 128:(tt + 1) * 128], i=papt:
                      e.tensor_copy(o, i)))

    if dbg and "xn2" in dbg:
        xn_tok = xn_tok + [dma(st, "sync", d["d_xn2"], xn_sb[:, :], waits=xn_tok)]
    if dbg and "cw" in dbg:
        t_cw_rows = t_cw_rows + [dma(st, "sync", d["d_cw"], cwtok[:, :],
                                     waits=t_cw_rows)]
    if dbg and "lg" in dbg:
        t_cw_rows = t_cw_rows + [dma(st, "sync", d["d_lg"], lg_sb[:, :],
                                     waits=t_cw_rows)]

    W1A, W3A, W2A = 0, 2048, 4096
    cwrep = sf[:, 0:2048]
    silu_b = [sf[:, 2048 + i * 256: 2048 + (i + 1) * 256].bitcast(BF16) for i in range(2)]
    ucw_b = [sf[:, 2560 + i * 256: 2560 + (i + 1) * 256].bitcast(BF16) for i in range(2)]

    w1_tok, w3_tok, w2_tok = {}, {}, {}
    t_wbufs = (pe, pe.n)
    w1_free = {0: t_wbufs, 1: t_wbufs}
    w3_free = {0: t_wbufs, 1: t_wbufs}
    w2_free = {0: t_wbufs, 1: t_wbufs}

    def pf_w1(widx, hc):
        par = hc % 2
        dst = w_sb[:, W1A + par * 1024: W1A + (par + 1) * 1024].rearrange(
            "p (kc n) -> p kc n", kc=KC)
        src = d["w1"][widx].rearrange("(kc p) hf -> p kc hf", p=128)[:, :, hc * 128:(hc + 1) * 128]
        w1_tok[(widx, hc)] = dma(wd_w1[par], "gpsimd", dst, src, waits=[w1_free[par]])

    def pf_w3(widx, hc):
        par = hc % 2
        dst = w_sb[:, W3A + par * 1024: W3A + (par + 1) * 1024].rearrange(
            "p (kc n) -> p kc n", kc=KC)
        src = d["w3"][widx].rearrange("(kc p) hf -> p kc hf", p=128)[:, :, hc * 128:(hc + 1) * 128]
        w3_tok[(widx, hc)] = dma(wd_w3[par], "gpsimd", dst, src, waits=[w3_free[par]])

    def pf_w2(widx, dc):
        par = dc % 2
        dst = w_sb[:, W2A + par * 2048: W2A + (par + 1) * 2048].rearrange(
            "p (hc n) -> p hc n", hc=16)
        src = d["w2"][widx].rearrange("(hc p) dm -> p hc dm", p=128)[:, :, dc * 128:(dc + 1) * 128]
        w2_tok[(widx, dc)] = dma(wd_w2[par], "gpsimd", dst, src, waits=[w2_free[par]])

    eo_mul_tok = [None, None]
    pf_w1(0, 0)
    pf_w3(0, 0)
    pf_w1(0, 1)
    pf_w3(0, 1)
    pf_w2(0, 0)
    pf_w2(0, 1)

    e_order = list(range(NE)) + [NE]
    for ei, e_ in enumerate(e_order):
        shared = (e_ == NE)
        widx = 8 if shared else e_
        nwidx = e_order[ei + 1] if ei + 1 < len(e_order) else None
        if not shared:
            if e_ == 0:
                cwsrc = csn[0:1, 0:2048]
                tok_cwsrc = list(t_cw_rows)
            else:
                tok_cp = dma(sw_cw, "sync", csn[0:1, 2048:4096], csn[e_:e_ + 1, 0:2048],
                             waits=[(pe, pe.n)] + t_cw_rows)
                cwsrc = csn[0:1, 2048:4096]
                tok_cwsrc = [tok_cp]
            for tsb in range(NTS):
                bank = ps[tsb % 2]
                bank_prod(bank,
                          lambda e, o=bank.ap[:, :],
                          r=cwsrc[0:1, tsb * 512:(tsb + 1) * 512]:
                          e.matmul(o, ones1b[:, :], r, start=True, stop=True),
                          waits=tok_cwsrc)
                bank_cons(bank, "vector",
                          lambda e, o=cwrep[:, tsb * 512:(tsb + 1) * 512], i=bank.ap[:, :]:
                          e.tensor_copy(o, i))
        t_cwrep = (dv, dv.n)

        for hc in range(16):
            par = hc % 2
            for tsp in range(2):
              for kc in range(KC):
                for tsx in (2 * tsp, 2 * tsp + 1):
                    bank = ps[4 * tsp + (tsx - 2 * tsp)]
                    col = kc * T + tsx * 512
                    bank_prod(bank,
                              lambda e, o=bank.ap[:, :],
                              l=w_sb[:, W1A + par * 1024 + kc * 128:
                                     W1A + par * 1024 + (kc + 1) * 128],
                              r=xn_sb[:, col:col + 512],
                              s0=(kc == 0), s1=(kc == KC - 1):
                              e.matmul(o, l, r, start=s0, stop=s1),
                              waits=[w1_tok[(widx, hc)]] + xn_tok,
                              last=(kc == KC - 1))
              for kc in range(KC):
                for tsx in (2 * tsp, 2 * tsp + 1):
                    bank = ps[4 * tsp + 2 + (tsx - 2 * tsp)]
                    col = kc * T + tsx * 512
                    bank_prod(bank,
                              lambda e, o=bank.ap[:, :],
                              l=w_sb[:, W3A + par * 1024 + kc * 128:
                                     W3A + par * 1024 + (kc + 1) * 128],
                              r=xn_sb[:, col:col + 512],
                              s0=(kc == 0), s1=(kc == KC - 1):
                              e.matmul(o, l, r, start=s0, stop=s1),
                              waits=[w3_tok[(widx, hc)]] + xn_tok,
                              last=(kc == KC - 1))
              for tsx in (2 * tsp, 2 * tsp + 1):
                i2 = tsx - 2 * tsp
                gb_, ub_ = ps[4 * tsp + i2], ps[4 * tsp + 2 + i2]
                sb_b, ub_b = silu_b[i2], ucw_b[i2]
                tsg = bank_cons(gb_, "scalar",
                                lambda e, o=sb_b, i=gb_.ap[:, :]: e.activation(o, i, AF.Silu),
                                waits=[eo_mul_tok[i2]], ch=ac)
                if shared:
                    tuc = bank_cons(ub_, "vector",
                                    lambda e, o=ub_b, i=ub_.ap[:, :]: e.tensor_copy(o, i))
                else:
                    tuc = bank_cons(ub_, "vector",
                                    lambda e, o=ub_b, i=ub_.ap[:, :],
                                    c=cwrep[:, tsx * 512:(tsx + 1) * 512]:
                                    e.tensor_mul(o, i, c),
                                    waits=[t_cwrep])
                eocol = hc * T + tsx * 512
                eo_mul_tok[i2] = b.emit("vector",
                       lambda e, o=big[:, eocol:eocol + 512], a=sb_b, bb=ub_b:
                       e.tensor_mul(o, a, bb),
                       waits=[tsg, tuc], incs=[(dv, 1)])[0]
            w1_free[par] = (pe, pe.n)
            w3_free[par] = (pe, pe.n)
            nxt = hc + 2
            if nxt < 16:
                pf_w1(widx, nxt)
                pf_w3(widx, nxt)
            elif nwidx is not None:
                pf_w1(nwidx, nxt - 16)
                pf_w3(nwidx, nxt - 16)
        t_eo = (dv, dv.n)

        for dc in range(KC):
            par = dc % 2
            bset = ps[4 * (dc % 2): 4 * (dc % 2) + 4]
            for hc in range(16):
                for tsx in range(NTS):
                    bank = bset[tsx]
                    eocol = hc * T + tsx * 512
                    bank_prod(bank,
                              lambda e, o=bank.ap[:, :],
                              l=w_sb[:, W2A + par * 2048 + hc * 128:
                                     W2A + par * 2048 + (hc + 1) * 128],
                              r=big[:, eocol:eocol + 512],
                              s0=(hc == 0), s1=(hc == 15):
                              e.matmul(o, l, r, start=s0, stop=s1),
                              waits=[w2_tok[(widx, dc)], t_eo],
                              last=(hc == 15))
            for tsx in range(NTS):
                bank = bset[tsx]
                xcol = dc * T + tsx * 512
                bank_cons(bank, "vector",
                          lambda e, o=x_sb[:, xcol:xcol + 512], i=bank.ap[:, :]:
                          e.tensor_add(o, o, i))
            w2_free[par] = (pe, pe.n)
            nxt = dc + 2
            if nxt < KC:
                pf_w2(widx, nxt)
            elif nwidx is not None:
                pf_w2(nwidx, nxt - KC)
    t_final = (dv, dv.n)

    out_tiles = d["out"].rearrange("(g p) m -> p g m", p=128)
    slot = 0
    for g4 in range(4):
        par = g4 % 2
        evac_toks = []
        for ti in range(4):
            tt = g4 * 4 + ti
            for kc in range(KC):
                bank = ps[slot % 2]
                sl = (slot // 2) % 4
                pap = bank.ap[:, sl * 128:(sl + 1) * 128]
                col = kc * T + tt * 128
                bank_prod(bank,
                          lambda e, o=pap, i=x_sb[:, col:col + 128]:
                          e.transpose(o, i, identf[:, :]),
                          waits=[t_final, stage_free[par]])
                evac_toks.append(
                    bank_cons(bank, "vector",
                              lambda e,
                              o=stage[par][:, ti * 1024 + kc * 128: ti * 1024 + (kc + 1) * 128],
                              i=pap: e.tensor_copy(o, i)))
                slot += 1
        tok_st = dma(st_out[par], "sync", out_tiles[:, g4 * 4:(g4 + 1) * 4, :],
                     stage[par].rearrange("p (g m) -> p g m", g=4),
                     waits=evac_toks)
        stage_free[par] = tok_st

    b.emit("sync", lambda e: e.nop(),
           waits=[(st_out[0], st_out[0].n), (st_out[1], st_out[1].n),
                  (st_dbg, st_dbg.n)])

    with nc.Block() as block:
        block.sync(lambda e: [f(e) for f in b.q["sync"]])
        block.gpsimd(lambda e: [f(e) for f in b.q["gpsimd"]])
        block.tensor(lambda e: [f(e) for f in b.q["tensor"]])
        block.vector(lambda e: [f(e) for f in b.q["vector"]])
        block.scalar(lambda e: [f(e) for f in b.q["scalar"]])

    return nc


_consts = None
_nc_cache = None


def kernel(L0_attn_x, L0_attn_w_aq, L0_attn_w_ak, L0_attn_w_av, L0_attn_w_ao,
           L0_ffn_router_w, L0_ffn_experts_bias, L0_ffn_w1, L0_ffn_w3, L0_ffn_w2,
           L0_ffn_sw1, L0_ffn_sw3, L0_ffn_sw2):
    global _consts, _nc_cache
    if _consts is None:
        _consts = host_consts()
    x = np.ascontiguousarray(np.asarray(L0_attn_x, dtype=np.float32))
    wq = np.ascontiguousarray(np.asarray(L0_attn_w_aq, dtype=np.float32).reshape(DM, DM))
    wk = np.ascontiguousarray(np.asarray(L0_attn_w_ak, dtype=np.float32).reshape(DM, DM))
    wv = np.ascontiguousarray(np.asarray(L0_attn_w_av, dtype=np.float32).reshape(DM, DM))
    wo = np.ascontiguousarray(np.asarray(L0_attn_w_ao, dtype=np.float32).reshape(DM, DM))
    rw = np.ascontiguousarray(np.asarray(L0_ffn_router_w, dtype=np.float32))
    eb = np.ascontiguousarray(np.asarray(L0_ffn_experts_bias, dtype=np.float32))
    w1 = np.ascontiguousarray(np.concatenate(
        [np.asarray(L0_ffn_w1, dtype=np.float32),
         np.asarray(L0_ffn_sw1, dtype=np.float32)], axis=0))
    w3 = np.ascontiguousarray(np.concatenate(
        [np.asarray(L0_ffn_w3, dtype=np.float32),
         np.asarray(L0_ffn_sw3, dtype=np.float32)], axis=0))
    w2 = np.ascontiguousarray(np.concatenate(
        [np.asarray(L0_ffn_w2, dtype=np.float32),
         np.asarray(L0_ffn_sw2, dtype=np.float32)], axis=0))

    if _nc_cache is None:
        _nc_cache = build_nc()
    nc = _nc_cache

    shared = {"wq": wq, "wk": wk, "wv": wv, "wo": wo, "rw": rw, "eb": eb,
              "w1": w1, "w3": w3, "w2": w2, **_consts}
    in_maps = [dict(shared, x=np.ascontiguousarray(x[bi])) for bi in range(8)]
    res = run_bass_kernel_spmd(nc, in_maps, core_ids=list(range(8)))
    return np.stack([np.asarray(r["out"], dtype=np.float32) for r in res.results], axis=0)


# version 16 (final)
# speedup vs baseline: 1.1980x; 1.1980x over previous
import numpy as np
import ml_dtypes
import concourse.bass as bass
import concourse.mybir as mybir
from concourse.bass_utils import run_bass_kernel_spmd
from contextlib import ExitStack

F32 = mybir.dt.float32
BF16 = mybir.dt.bfloat16
AF = mybir.ActivationFunctionType
ALU = mybir.AluOpType
AX = mybir.AxisListType

T = 2048
DM = 1024
NH = 8
DH = 128
NE = 8
HF = 2048
KC = DM // 128
NTS = T // 512
NTT = T // 128
NEG = -1e30
EPS = 1e-5

ENGS = ("sync", "gpsimd", "tensor", "vector", "scalar")


class Ch:
    __slots__ = ("sem", "n")

    def __init__(self, sem):
        self.sem = sem
        self.n = 0


class Builder:

    def __init__(self):
        self.q = {e: [] for e in ENGS}
        self.lastw = {e: {} for e in ENGS}
        self.chain = {}
        self.prev_tok = {}

    def emit(self, eng, fn, waits=(), incs=(), nochain=False):
        lw = self.lastw[eng]
        incs = list(incs)
        cc = self.chain.get(eng)
        if nochain:
            cc = None
        if cc is not None:
            waits = list(waits) + ([self.prev_tok[eng]] if self.prev_tok.get(eng) else [])
            if not any(ch is cc for ch, _ in incs):
                incs.append((cc, 1))
        ws = []
        for tok in waits:
            if tok is None:
                continue
            ch, v = tok
            k = id(ch)
            if lw.get(k, 0) < v:
                ws.append((ch.sem, v))
                lw[k] = v
        toks = []
        rincs = []
        for ch, amt in incs:
            ch.n += amt
            toks.append((ch, ch.n))
            rincs.append((ch.sem, amt))
        if cc is not None:
            for ch, _ in incs:
                if ch is cc:
                    self.prev_tok[eng] = (cc, cc.n)

        def run(e, ws=ws, fn=fn, rincs=rincs):
            for sem, v in ws:
                e.wait_ge(sem, v)
            inst = fn(e)
            for sem, amt in rincs:
                inst = inst.then_inc(sem, amt)

        self.q[eng].append(run)
        return toks


class PsBank:
    __slots__ = ("ap", "ready", "frees")

    def __init__(self, ap):
        self.ap = ap
        self.ready = None
        self.frees = []


def host_consts():
    pos = np.arange(T, dtype=np.float64)
    dims = np.arange(64, dtype=np.float64)
    freq = 10000.0 ** (-dims / 64.0)
    rad = pos[None, :] * freq[:, None]
    cs = np.cos(rad)
    sn = np.sin(rad)
    cs_full = np.concatenate([cs, cs], 0).astype(np.float32)
    sn_signed = np.concatenate([-sn, sn], 0).astype(np.float32)
    r = np.arange(128)[:, None]
    u = np.arange(896)[None, :]
    maskM = np.where(u < r + 384, NEG, 0.0).astype(ml_dtypes.bfloat16)
    P = np.zeros((128, 128), dtype=np.float32)
    for rr in range(64):
        P[rr, rr + 64] = -1.0
        P[rr + 64, rr] = 1.0
    pmat = np.ascontiguousarray(P.T).astype(ml_dtypes.bfloat16)
    identb = np.eye(128, dtype=ml_dtypes.bfloat16)
    identf = np.eye(128, dtype=np.float32)
    return {"c_cs": cs_full, "c_sn": sn_signed, "c_mask": maskM,
            "c_identb": identb, "c_identf": identf, "c_pmat": pmat}


def build_nc(dbg=False):
    nc = bass.Bass("TRN2", target_bir_lowering=False, debug=False)

    d = {}
    d["x"] = nc.dram_tensor("x", [T, DM], F32, kind="ExternalInput").ap()
    d["wq"] = nc.dram_tensor("wq", [DM, DM], F32, kind="ExternalInput").ap()
    d["wk"] = nc.dram_tensor("wk", [DM, DM], F32, kind="ExternalInput").ap()
    d["wv"] = nc.dram_tensor("wv", [DM, DM], F32, kind="ExternalInput").ap()
    d["wo"] = nc.dram_tensor("wo", [DM, DM], F32, kind="ExternalInput").ap()
    d["rw"] = nc.dram_tensor("rw", [NE, DM], F32, kind="ExternalInput").ap()
    d["eb"] = nc.dram_tensor("eb", [NE], F32, kind="ExternalInput").ap()
    d["w1"] = nc.dram_tensor("w1", [NE + 1, DM, HF], F32, kind="ExternalInput").ap()
    d["w3"] = nc.dram_tensor("w3", [NE + 1, DM, HF], F32, kind="ExternalInput").ap()
    d["w2"] = nc.dram_tensor("w2", [NE + 1, HF, DM], F32, kind="ExternalInput").ap()
    d["c_cs"] = nc.dram_tensor("c_cs", [128, T], F32, kind="ExternalInput").ap()
    d["c_sn"] = nc.dram_tensor("c_sn", [128, T], F32, kind="ExternalInput").ap()
    d["c_mask"] = nc.dram_tensor("c_mask", [128, 896], BF16, kind="ExternalInput").ap()
    d["c_identb"] = nc.dram_tensor("c_identb", [128, 128], BF16, kind="ExternalInput").ap()
    d["c_identf"] = nc.dram_tensor("c_identf", [128, 128], F32, kind="ExternalInput").ap()
    d["c_pmat"] = nc.dram_tensor("c_pmat", [128, 128], BF16, kind="ExternalInput").ap()
    d["out"] = nc.dram_tensor("out", [T, DM], F32, kind="ExternalOutput").ap()
    if dbg:
        dbg = set(dbg) if not isinstance(dbg, bool) else {"xn1","xattn","o","v","cw","xn2","qk"}
        d["d_xn1"] = nc.dram_tensor("d_xn1", [128, KC * T], BF16, kind="ExternalOutput").ap()
        d["d_xattn"] = nc.dram_tensor("d_xattn", [128, KC * T], F32, kind="ExternalOutput").ap()
        d["d_o"] = nc.dram_tensor("d_o", [128, 16384], BF16, kind="ExternalOutput").ap()
        d["d_v"] = nc.dram_tensor("d_v", [128, 8192], BF16, kind="ExternalOutput").ap()
        d["d_cw"] = nc.dram_tensor("d_cw", [128, 128], F32, kind="ExternalOutput").ap()
        d["d_xn2"] = nc.dram_tensor("d_xn2", [128, KC * T], BF16, kind="ExternalOutput").ap()
        d["d_qk"] = nc.dram_tensor("d_qk", [128, 8192], BF16, kind="ExternalOutput").ap()
        d["d_lg"] = nc.dram_tensor("d_lg", [128, 128], F32, kind="ExternalOutput").ap()
        d["d_km"] = nc.dram_tensor("d_km", [128, 48], F32, kind="ExternalOutput").ap()

    x_sb = nc.alloc_sbuf_tensor("x_sb", [128, KC * T], F32)
    xn_sb = nc.alloc_sbuf_tensor("xn_sb", [128, KC * T], BF16)
    big = nc.alloc_sbuf_tensor("big", [128, 32768], BF16)
    w_sb = nc.alloc_sbuf_tensor("w_sb", [128, 8192], BF16)
    csn = nc.alloc_sbuf_tensor("csn", [128, 4096], BF16)
    s_sb = nc.alloc_sbuf_tensor("s_sb", [128, 3072], F32)
    maskm = nc.alloc_sbuf_tensor("maskm", [128, 896], BF16)
    pT = nc.alloc_sbuf_tensor("pT", [128, 1024], BF16)
    identb = nc.alloc_sbuf_tensor("identb", [8, 16], BF16)
    identf = nc.alloc_sbuf_tensor("identf", [128, 128], F32)
    pmat_sb = nc.alloc_sbuf_tensor("pmat_sb", [128, 128], BF16)
    onesb = nc.alloc_sbuf_tensor("onesb", [128, 1], BF16)
    ones1 = nc.alloc_sbuf_tensor("ones1", [1, 128], F32)
    ones1b = nc.alloc_sbuf_tensor("ones1b", [1, 128], BF16)
    rwT = nc.alloc_sbuf_tensor("rwT", [128, KC * 8], BF16)
    rw_raw = nc.alloc_sbuf_tensor("rw_raw", [8, DM], BF16)
    bias_rep = nc.alloc_sbuf_tensor("bias_rep", [128, 8], F32)
    cwtok = nc.alloc_sbuf_tensor("cwtok", [128, 128], F32)
    lg_sb = nc.alloc_sbuf_tensor("lg_sb", [128, 128], F32)
    m_acc = nc.alloc_sbuf_tensor("m_acc", [128, 1], F32)
    m_all = nc.alloc_sbuf_tensor("m_all", [128, 40], F32)
    ta64 = nc.alloc_sbuf_tensor("ta64", [128, 64], F32)
    m1a = nc.alloc_sbuf_tensor("m1a", [128, 16], F32)
    m2a = nc.alloc_sbuf_tensor("m2a", [128, 16], F32)
    kd128 = nc.alloc_sbuf_tensor("kd128", [128, 128], F32)
    km1a = nc.alloc_sbuf_tensor("km1a", [128, 128], F32)
    km2a = nc.alloc_sbuf_tensor("km2a", [128, 128], F32)
    mska = nc.alloc_sbuf_tensor("mska", [128, 128], F32)
    zer128 = nc.alloc_sbuf_tensor("zer128", [128, 128], F32)
    d12a = nc.alloc_sbuf_tensor("d12a", [128, 16], F32)
    p1a = nc.alloc_sbuf_tensor("p1a", [128, 16], F32)
    p2a = nc.alloc_sbuf_tensor("p2a", [128, 16], F32)
    m_tmp = nc.alloc_sbuf_tensor("m_tmp", [128, 1], F32)
    negmh = nc.alloc_sbuf_tensor("negmh", [128, 1], F32)
    mh_sb = nc.alloc_sbuf_tensor("mh_sb", [1, 8], F32)

    bigf = big[:, :].bitcast(F32)
    csnf = csn[:, :].bitcast(F32)
    sf = s_sb

    ps = [PsBank(nc.alloc_psum_tensor(f"ps{i}", [128, 512], F32).ap()) for i in range(8)]

    stack = ExitStack()
    mk = lambda name: Ch(stack.enter_context(nc.semaphore(name)))
    ld = mk("ldc")
    ldg = mk("ldg")
    ld_stage = [mk("lds0"), mk("lds1")]
    sw_rope = [mk("swr0"), mk("swr1")]
    sw_cw = mk("swcw")
    st_out = [mk("sto0"), mk("sto1")]
    st_dbg = mk("stdbg")
    wd_qk = [mk("wqk0"), mk("wqk1")]
    wd_wv = mk("wwv")
    wd_wo = [mk("wwo0"), mk("wwo1")]
    wd_w1 = [mk("ww10"), mk("ww11")]
    wd_w3 = [mk("ww30"), mk("ww31")]
    wd_w2 = [mk("ww20"), mk("ww21")]
    pe = mk("pe")
    dv = mk("dv")
    ac = mk("ac")
    gp = mk("gp")
    st = st_dbg
    nc._ch_stack = stack

    b = Builder()
    b.chain["vector"] = dv
    b.chain["scalar"] = ac
    b.chain["gpsimd"] = gp

    def dma(ch, eng, out_ap, in_ap, waits=()):
        return b.emit(eng, lambda e, o=out_ap, i=in_ap: e.dma_start(out=o, in_=i),
                      waits=waits, incs=[(ch, 16)], nochain=True)[0]

    def bank_prod(bank, fn, waits=(), last=True):
        w = list(waits) + bank.frees
        toks = b.emit("tensor", fn, waits=w, incs=[(pe, 1)] if last else ())
        bank.frees = []
        if last:
            bank.ready = toks[0]
            return toks[0]
        return None

    def bank_cons(bank, eng, fn, waits=(), ch=dv):
        w = list(waits) + ([bank.ready] if bank.ready else [])
        tok = b.emit(eng, fn, waits=w, incs=[(ch, 1)])[0]
        bank.frees.append(tok)
        return tok

    t_identb = dma(ld, "sync", identb[0:8, 0:16], d["c_identb"][0:8, 0:16])
    t_identf = dma(ld, "sync", identf[:, :], d["c_identf"])
    t_mask = dma(ld, "sync", maskm[:, :], d["c_mask"])
    t_eb = dma(ld, "sync", sf[0:1, 0:8], d["eb"].rearrange("(a b) -> a b", a=1))
    t_pmat = dma(ld, "sync", pmat_sb[:, :], d["c_pmat"])
    t_rw = dma(ldg, "gpsimd", rw_raw[:, :], d["rw"])
    t_cs = dma(ldg, "gpsimd", csn[:, 0:T], d["c_cs"])
    t_sn = dma(ldg, "gpsimd", csn[:, T:2 * T], d["c_sn"])

    t_consts = (ld, ld.n)
    t_constsg = (ldg, ldg.n)
    t_identb = t_identf = t_mask = t_eb = t_pmat = t_consts
    t_rw = t_cs = t_sn = t_constsg
    b.emit("vector", lambda e: e.memset(onesb[:, :], 1.0), incs=[(dv, 1)])
    b.emit("vector", lambda e: e.memset(zer128[:, :], 0.0), incs=[(dv, 1)])
    t_ones1 = b.emit("vector", lambda e: e.memset(ones1[:, :], 1.0), incs=[(dv, 1)])[0]
    b.emit("vector", lambda e: e.memset(ones1b[:, :], 1.0), incs=[(dv, 1)])

    WQ0, WK0, WVG = 0, 2048, 4096
    wq_flat = d["wq"].rearrange("(kc p) n -> p kc n", p=128)
    wk_flat = d["wk"].rearrange("(kc p) n -> p kc n", p=128)
    wv_flat = d["wv"].rearrange("(kc p) n -> p kc n", p=128)
    wo_flat = d["wo"].rearrange("(c p) m -> p c m", p=128)

    def wq_slice(par):
        return w_sb[:, WQ0 + par * 1024: WQ0 + (par + 1) * 1024]

    def wk_slice(par):
        return w_sb[:, WK0 + par * 1024: WK0 + (par + 1) * 1024]

    wv_group = w_sb[:, WVG:WVG + 4096]

    t_wq = [None] * NH
    t_wk = [None] * NH
    t_wv = [None, None]

    def prefetch_qk(h, waits=()):
        par = h % 2
        dq = wq_slice(par).rearrange("p (kc n) -> p kc n", kc=KC)
        dma(wd_qk[par], "gpsimd", dq, wq_flat[:, :, h * DH:(h + 1) * DH], waits=waits)
        dk = wk_slice(par).rearrange("p (kc n) -> p kc n", kc=KC)
        tok = dma(wd_qk[par], "gpsimd", dk, wk_flat[:, :, h * DH:(h + 1) * DH])
        t_wq[h] = t_wk[h] = tok

    def prefetch_wv(half, waits=()):
        dvv = wv_group.rearrange("p (kc n) -> p kc n", kc=KC)
        t_wv[half] = dma(wd_wv, "gpsimd", dvv, wv_flat[:, :, half * 512:(half + 1) * 512],
                         waits=waits)

    prefetch_wv(0)
    prefetch_qk(0)
    prefetch_qk(1)

    x_tiles = d["x"].rearrange("(g p) m -> p g m", p=128)
    t_xgrp = [None] * 4
    stage = [bigf[:, 0:4096], bigf[:, 4096:8192]]
    stage_free = [None, None]

    slot = 0
    for g4 in range(4):
        par = g4 % 2
        wgrp = [stage_free[par]] if stage_free[par] else []
        tok_ld = dma(ld_stage[par], "sync",
                     stage[par].rearrange("p (g m) -> p g m", g=4),
                     x_tiles[:, g4 * 4:(g4 + 1) * 4, :], waits=wgrp)
        for ti in range(4):
            tt = g4 * 4 + ti
            for kc in range(KC):
                src = stage[par][:, ti * 1024 + kc * 128: ti * 1024 + (kc + 1) * 128]
                bank = ps[slot % 2]
                sl = (slot // 2) % 4
                pap = bank.ap[:, sl * 128:(sl + 1) * 128]
                bank_prod(bank, lambda e, o=pap, i=src: e.transpose(o, i, identf[:, :]),
                          waits=[tok_ld, t_identf])
                dstc = kc * T + tt * 128
                bank_cons(bank, "vector",
                          lambda e, o=x_sb[:, dstc:dstc + 128], i=pap: e.tensor_copy(o, i))
                slot += 1
        stage_free[par] = (pe, pe.n)
        t_xgrp[g4] = (dv, dv.n)
    t_x_ready = (dv, dv.n)

    for kc in range(KC):
        bank = ps[2]
        pap = bank.ap[:, 0:4].bitcast(BF16)
        bank_prod(bank, lambda e, o=pap, i=rw_raw[:, kc * 128:(kc + 1) * 128]:
                  e.transpose(o, i, identb[0:8, 0:8]),
                  waits=[t_rw, t_identb])
        bank_cons(bank, "vector",
                  lambda e, o=rwT[:, kc * 8:(kc + 1) * 8], i=pap: e.tensor_copy(o, i))
    bank = ps[3]
    pap = bank.ap[:, 0:8]
    bank_prod(bank, lambda e, o=pap: e.matmul(o, ones1[:, :], sf[0:1, 0:8],
                                              start=True, stop=True),
              waits=[t_eb, t_ones1])
    bank_cons(bank, "vector",
              lambda e, o=bias_rep[:, :], i=pap: e.tensor_copy(o, i))

    def layernorm(t_x_tok, per_ts=None):
        if not isinstance(t_x_tok, list):
            t_x_tok = [t_x_tok]
        xn_tok = []
        U0 = sf[0:1, 2048:2560]
        U1 = sf[0:1, 2560:3072]
        U2 = sf[0:1, 1536:2048]
        a_rep = sf[:, 0:512]
        b_rep = sf[:, 512:1024]
        tmp = sf[:, 1024:1536]
        t_gp_prev = None
        for ts in range(NTS):
            tcol = ts * 512
            t_cp = None
            for kc in range(KC):
                col = kc * T + tcol
                t_cp = b.emit("vector",
                              lambda e, o=xn_sb[:, col:col + 512],
                              i=x_sb[:, col:col + 512]: e.tensor_copy(o, i),
                              waits=(per_ts[ts] if per_ts else t_x_tok),
                              incs=[(dv, 1)])[0]
            bank_s = ps[0]
            for kc in range(KC):
                col = kc * T + tcol
                bank_prod(bank_s,
                          lambda e, r=xn_sb[:, col:col + 512],
                          s0=(kc == 0), s1=(kc == KC - 1):
                          e.matmul(bank_s.ap[0:1, :], onesb[:, :], r, start=s0, stop=s1),
                          waits=[t_cp], last=(kc == KC - 1))
            t_sum = bank_s.ready
            xsq = [sf[:, 2560 + i * 256: 2560 + (i + 1) * 256].bitcast(BF16)
                   for i in range(2)]
            bank_q = ps[1]
            t_xsq_pe = [None, None]
            for kc in range(KC):
                xa = xsq[kc % 2]
                col = kc * T + tcol
                w = [t_cp, t_xsq_pe[kc % 2]]
                t_sq = b.emit("vector",
                              lambda e, o=xa, i=xn_sb[:, col:col + 512]: e.tensor_mul(o, i, i),
                              waits=w, incs=[(dv, 1)])[0]
                tokp = bank_prod(bank_q,
                                 lambda e, r=xa, s0=(kc == 0), s1=(kc == KC - 1):
                                 e.matmul(bank_q.ap[0:1, :], onesb[:, :], r, start=s0, stop=s1),
                                 waits=[t_sq], last=True)
                t_xsq_pe[kc % 2] = tokp
            t_ssq = bank_q.ready
            t1 = b.emit("vector", lambda e: e.tensor_scalar_mul(U0, bank_s.ap[0:1, :], 1.0 / DM),
                        waits=[t_sum], incs=[(dv, 1)])[0]
            bank_s.frees.append(t1)
            t2 = b.emit("vector", lambda e: e.tensor_scalar_mul(U1, bank_q.ap[0:1, :], 1.0 / DM),
                        waits=[t_ssq], incs=[(dv, 1)])[0]
            bank_q.frees.append(t2)
            b.emit("vector", lambda e: e.tensor_mul(U2, U0, U0), incs=[(dv, 1)])
            b.emit("vector", lambda e: e.tensor_sub(U1, U1, U2), incs=[(dv, 1)])
            t3 = b.emit("vector", lambda e: e.tensor_scalar_add(U1, U1, EPS),
                        incs=[(dv, 1)])[0]
            t4 = b.emit("scalar", lambda e: e.activation(U2, U1, AF.Ln),
                        waits=[t3], incs=[(ac, 1)])[0]
            t5 = b.emit("scalar", lambda e: e.activation(U1, U2, AF.Exp, scale=-0.5),
                        waits=[t4], incs=[(ac, 1)])[0]
            t6 = b.emit("vector", lambda e: e.tensor_scalar_mul(U2, U0, -1.0),
                        waits=[t5], incs=[(dv, 1)])[0]
            t7 = b.emit("vector", lambda e: e.tensor_mul(U0, U2, U1), incs=[(dv, 1)])[0]
            bank_a, bank_b = ps[2], ps[3]
            bank_prod(bank_a, lambda e: e.matmul(bank_a.ap[:, :], ones1[:, :], U1,
                                                 start=True, stop=True),
                      waits=[t5, t6, t_ones1])
            bank_prod(bank_b, lambda e: e.matmul(bank_b.ap[:, :], ones1[:, :], U0,
                                                 start=True, stop=True),
                      waits=[t7])
            ta = bank_cons(bank_a, "vector", lambda e: e.tensor_copy(a_rep, bank_a.ap[:, :]),
                           waits=[t_gp_prev])
            tb = bank_cons(bank_b, "vector", lambda e: e.tensor_copy(b_rep, bank_b.ap[:, :]))
            t_last = None
            for kc in range(KC):
                col = kc * T + tcol
                tm = b.emit("gpsimd",
                            lambda e, i=x_sb[:, col:col + 512]: e.tensor_mul(tmp, i, a_rep),
                            waits=[ta, t_sum, t_ssq, t_last], incs=[(gp, 1)])[0]
                t_last = b.emit("gpsimd",
                                lambda e, o=xn_sb[:, col:col + 512]: e.tensor_add(o, tmp, b_rep),
                                waits=[tb, tm], incs=[(gp, 1)])[0]
            t_gp_prev = t_last
            xn_tok.append(t_last)
        return xn_tok

    dbg_x0_tok = []
    if dbg and "x0" in dbg:
        dbg_x0_tok.append(dma(st, "sync", d["d_xattn"], x_sb[:, :], waits=[t_x_ready]))
    xn_tok = layernorm(t_x_ready, per_ts=[[t] for t in t_xgrp])
    dbg_gate_x2 = []
    if dbg and "xn1" in dbg:
        dbg_gate_x2.append(dma(st, "sync", d["d_xn1"], xn_sb[:, :], waits=xn_tok))

    QB = 24576
    o_col = 8192

    def q_slice(par):
        return big[:, QB + par * 4096: QB + par * 4096 + 2048]

    def k_slice(par):
        return big[:, QB + par * 4096 + 2048: QB + par * 4096 + 4096]

    sbf = s_sb[:, :].bitcast(BF16)
    rope_qb = [sbf[:, i * 512:(i + 1) * 512] for i in range(2)]
    rope_t1 = [sf[:, 1024 + i * 512: 1024 + (i + 1) * 512] for i in range(2)]
    scale_rep = sf[:, 2048:2560]
    invl_row = sf[0:1, 2560:3072]
    mh_row = mh_sb[0:1, 0:8]
    qb_free_tok = [None] * 2
    t_rot_prev = [None]

    qk_free = [None, None]
    t_v_done = [None, None]

    def emit_v(half):
        for tt in range(NTT):
            bank = ps[tt % 2]
            for kc in range(KC):
                lhs = xn_sb[:, kc * T + tt * 128: kc * T + (tt + 1) * 128]
                bank_prod(bank,
                          lambda e, l=lhs, r=wv_group[:, kc * 512:(kc + 1) * 512],
                          o=bank.ap[:, :], s0=(kc == 0), s1=(kc == KC - 1):
                          e.matmul(o, l, r, start=s0, stop=s1),
                          waits=[t_wv[half]] + xn_tok, last=(kc == KC - 1))
            bank_cons(bank, "vector",
                      lambda e, o=big[:, tt * 512:(tt + 1) * 512], i=bank.ap[:, :]:
                      e.tensor_copy(o, i))
        t_v_done[half] = (dv, dv.n)
        if half == 0:
            prefetch_wv(1, waits=[(pe, pe.n)])

    def emit_qkv_rope(h):
        par = h % 2
        for which in range(2):
            wslice = wq_slice(par) if which == 0 else wk_slice(par)
            t_w = t_wq[h]
            dst = q_slice(par) if which == 0 else k_slice(par)
            for kc in range(KC):
                for ts in range(NTS):
                    bank = ps[ts]
                    col = kc * T + ts * 512
                    bank_prod(bank,
                              lambda e, o=bank.ap[:, :],
                              l=wslice[:, kc * 128:(kc + 1) * 128],
                              r=xn_sb[:, col:col + 512],
                              s0=(kc == 0), s1=(kc == KC - 1):
                              e.matmul(o, l, r, start=s0, stop=s1),
                              waits=[t_w, xn_tok[ts], qk_free[par]],
                              last=(kc == KC - 1))
            for ts in range(NTS):
                sl = ts % 2
                bank = ps[ts]
                pqb = ps[4 + sl]
                qb = rope_qb[sl]
                tcs = csn[:, ts * 512:(ts + 1) * 512]
                tsn = csn[:, T + ts * 512: T + ts * 512 + 512]
                te = bank_cons(bank, "vector",
                               lambda e, o=qb, i=bank.ap[:, :]: e.tensor_copy(o, i),
                               waits=[qb_free_tok[sl]])
                qb_free_tok[sl] = bank_prod(pqb,
                                            lambda e, o=pqb.ap[:, :], r=qb:
                                            e.matmul(o, pmat_sb[:, :], r,
                                                     start=True, stop=True),
                                            waits=[te, t_pmat])
                t_t1a = bank_cons(bank, "vector",
                          lambda e, o=rope_t1[0], i=bank.ap[:, :], c=tcs:
                          e.tensor_mul(o, i, c),
                          waits=[t_cs, t_rot_prev[0]])
                t_t1b = bank_cons(pqb, "vector",
                          lambda e, o=rope_t1[1], i=pqb.ap[:, :], c=tsn:
                          e.tensor_mul(o, i, c),
                          waits=[t_sn, t_rot_prev[0]])
                t_rot_prev[0] = b.emit("gpsimd",
                       lambda e, o=dst[:, ts * 512:(ts + 1) * 512]:
                       e.tensor_add(o, rope_t1[0], rope_t1[1]),
                       waits=[t_t1a, t_t1b], incs=[(gp, 1)])[0]
        return [(dv, dv.n), (gp, gp.n)]

    rope_done = [None] * NH
    pT_free = [None]

    def emit_passA(h):
        par = h % 2
        qs = q_slice(par)
        ks = k_slice(par)
        sq_scr = [pT[:, 0:512], pT[:, 512:1024]]
        nb = [ps[6], ps[7]]
        for which in range(2):
            src = qs if which == 0 else ks
            bank = nb[which]
            t_prev_mm = pT_free[0]
            for ts in range(NTS):
                tsq = b.emit("gpsimd",
                             lambda e, o=sq_scr[which],
                             i=src[:, ts * 512:(ts + 1) * 512]: e.tensor_mul(o, i, i),
                             waits=rope_done[h] + [t_prev_mm], incs=[(gp, 1)])[0]
                t_prev_mm = bank_prod(bank,
                                      lambda e, o=bank.ap[0:1, :], r=sq_scr[which],
                                      s0=(ts == 0), s1=(ts == NTS - 1):
                                      e.matmul(o, onesb[:, :], r, start=s0, stop=s1),
                                      waits=[tsq], last=True)
            bank_cons(bank, "vector",
                      lambda e, o=mh_row[0:1, which:which + 1], i=bank.ap[0:1, :]:
                      e.reduce_max(o, i, axis=AX.X))
        b.emit("vector",
               lambda e: e.tensor_mul(mh_row[0:1, 2:3], mh_row[0:1, 0:1],
                                      mh_row[0:1, 1:2]),
               incs=[(dv, 1)])
        t_mm2 = b.emit("vector",
               lambda e: e.tensor_scalar_add(mh_row[0:1, 2:3], mh_row[0:1, 2:3], 1e-30),
               incs=[(dv, 1)])[0]
        t_ln = b.emit("scalar",
                      lambda e: e.activation(mh_row[0:1, 3:4], mh_row[0:1, 2:3], AF.Ln),
                      waits=[t_mm2], incs=[(ac, 1)])[0]
        t_bd = b.emit("scalar",
                      lambda e: e.activation(mh_row[0:1, 2:3], mh_row[0:1, 3:4], AF.Exp,
                                             scale=0.5),
                      incs=[(ac, 1)])[0]
        tneg = b.emit("vector",
                      lambda e: e.tensor_scalar_mul(mh_row[0:1, 3:4], mh_row[0:1, 2:3],
                                                    -1.02 / DH),
                      waits=[t_bd], incs=[(dv, 1)])[0]
        bank = ps[6]
        bank_prod(bank, lambda e, o=bank.ap[:, 0:1]:
                  e.matmul(o, ones1[:, :], mh_row[0:1, 3:4], start=True, stop=True),
                  waits=[tneg])
        return bank_cons(bank, "vector",
                         lambda e, o=negmh[:, :], i=bank.ap[:, 0:1]: e.tensor_copy(o, i),
                         waits=[(ac, ac.n)])

    def emit_passB(h, t_negmh):
        par = h % 2
        half = h // 4
        hh = h % 4
        qs = q_slice(par)
        ks = k_slice(par)
        for qc in range(NTS):
            ob = ps[6]
            lb = ps[7]
            nkt = 4 * qc + 4
            pend = None

            def emit_ol(item, s1):
                tex_, vsl_, ptb_, kt_ = item
                bank_prod(ob,
                          lambda e, o=ob.ap[:, :], l=vsl_, r=ptb_,
                          s0=(kt_ == 0), s1_=s1:
                          e.matmul(o, l, r, start=s0, stop=s1_),
                          waits=[tex_, t_v_done[half]], last=s1)
                bank_prod(lb,
                          lambda e, o=lb.ap[0:1, :], r=ptb_,
                          s0=(kt_ == 0), s1_=s1:
                          e.matmul(o, onesb[:, :], r, start=s0, stop=s1_),
                          waits=[tex_], last=s1)

            for kt in range(nkt):
                sb_ = ps[4 + (kt % 2)]
                bank_prod(sb_,
                          lambda e, o=sb_.ap[:, :],
                          l=ks[:, kt * 128:(kt + 1) * 128],
                          r=qs[:, qc * 512:(qc + 1) * 512]:
                          e.matmul(o, l, r, start=True, stop=True),
                          waits=rope_done[h])
                if kt >= 4 * qc:
                    moff = 384 - 128 * (kt - 4 * qc)
                    tmk = bank_cons(sb_, "vector",
                                    lambda e, o=sb_.ap[:, :],
                                    m=maskm[:, moff:moff + 512]: e.tensor_add(o, o, m),
                                    waits=[t_mask])
                    sb_.ready = tmk
                ptb = pT[:, (kt % 2) * 512: (kt % 2) * 512 + 512]
                tex = bank_cons(sb_, "scalar",
                                lambda e, o=ptb, i=sb_.ap[:, :]:
                                e.activation(o, i, AF.Exp, bias=negmh[:, 0:1],
                                             scale=1.0 / DH),
                                waits=[t_negmh], ch=ac)
                vsl = big[:, kt * 512 + hh * 128: kt * 512 + hh * 128 + 128]
                if pend is not None:
                    emit_ol(pend, s1=False)
                pend = (tex, vsl, ptb, kt)
            emit_ol(pend, s1=True)
            til = bank_cons(lb, "vector",
                            lambda e, o=invl_row, i=lb.ap[0:1, :]: e.reciprocal(o, i))
            bank5 = ps[5]
            bank_prod(bank5,
                      lambda e, o=bank5.ap[:, :]:
                      e.matmul(o, ones1[:, :], invl_row, start=True, stop=True),
                      waits=[til])
            tsc = bank_cons(bank5, "vector",
                            lambda e, o=scale_rep, i=bank5.ap[:, :]: e.tensor_copy(o, i))
            ocol = o_col + h * T + qc * 512
            bank_cons(ob, "vector",
                      lambda e, o=big[:, ocol:ocol + 512], i=ob.ap[:, :]:
                      e.tensor_mul(o, i, scale_rep),
                      waits=[tsc])
        qk_free[par] = (pe, pe.n)
        pT_free[0] = (pe, pe.n)

    emit_v(0)
    rope_done[0] = emit_qkv_rope(0)
    for h in range(NH):
        t_negmh = emit_passA(h)
        if h + 1 < NH:
            if h + 1 == 4:
                emit_v(1)
            rope_done[h + 1] = emit_qkv_rope(h + 1)
            if h + 2 < NH:
                prefetch_qk(h + 2, waits=[(pe, pe.n)])
        emit_passB(h, t_negmh)

    t_o_all = (dv, dv.n)

    def wo_slice(par):
        return w_sb[:, par * 1024: (par + 1) * 1024]

    t_wo_mc = [None] * KC
    for mc in range(2):
        dwo = wo_slice(mc % 2).rearrange("p (c n) -> p c n", c=KC)
        t_wo_mc[mc] = dma(wd_wo[mc % 2], "gpsimd", dwo, wo_flat[:, :, mc * 128:(mc + 1) * 128],
                          waits=[qk_free[mc % 2]])
    for mc in range(KC):
        par = mc % 2
        for ts in range(NTS):
            bank = ps[(mc * NTS + ts) % 4]
            for c in range(KC):
                ocol2 = o_col + c * T + ts * 512
                bank_prod(bank,
                          lambda e, o=bank.ap[:, :],
                          l=wo_slice(par)[:, c * 128:(c + 1) * 128],
                          r=big[:, ocol2:ocol2 + 512],
                          s0=(c == 0), s1=(c == KC - 1):
                          e.matmul(o, l, r, start=s0, stop=s1),
                          waits=[t_wo_mc[mc], t_o_all], last=(c == KC - 1))
            xcol = mc * T + ts * 512
            bank_cons(bank, "vector",
                      lambda e, o=x_sb[:, xcol:xcol + 512], i=bank.ap[:, :]:
                      e.tensor_add(o, o, i),
                      waits=dbg_x0_tok)
        if mc + 2 < KC:
            dwo = wo_slice(par).rearrange("p (c n) -> p c n", c=KC)
            t_wo_mc[mc + 2] = dma(wd_wo[par], "gpsimd", dwo,
                                  wo_flat[:, :, (mc + 2) * 128:(mc + 3) * 128],
                                  waits=[(pe, pe.n)])
    t_x2 = (dv, dv.n)
    if dbg and "o" in dbg:
        dbg_gate_x2.append(dma(st, "sync", d["d_o"], big[:, 8192:24576],
                               waits=[t_o_all]))
    if dbg and "xattn" in dbg:
        dbg_gate_x2.append(dma(st, "sync", d["d_xattn"], x_sb[:, :], waits=[t_x2]))

    xn_tok = layernorm([t_x2] + dbg_gate_x2)

    t_cw_rows = []
    t_lg = []
    for tt in range(NTT):
        bank = ps[4 + tt % 2]
        for kc in range(KC):
            bank_prod(bank,
                      lambda e, o=bank.ap[:, 0:8],
                      l=xn_sb[:, kc * T + tt * 128: kc * T + (tt + 1) * 128],
                      r=rwT[:, kc * 8:(kc + 1) * 8],
                      s0=(kc == 0), s1=(kc == KC - 1):
                      e.matmul(o, l, r, start=s0, stop=s1),
                      waits=[xn_tok[tt // 4]], last=(kc == KC - 1))
        t_lg.append(bank_cons(bank, "vector",
                              lambda e, i=bank.ap[:, 0:8],
                              o=lg_sb[:, tt * 8:(tt + 1) * 8]:
                              e.tensor_add(o, i, bias_rep[:, :])))
    lg3 = lg_sb[:, :].rearrange("p (g e) -> p g e", e=8)
    ta3 = ta64[:, :].rearrange("p (g e) -> p g e", e=4)

    def vnop():
        pass

    def m1bc(t):
        return t[:, :].rearrange("p (g o) -> p g o", o=1).broadcast_to([128, 16, 8])

    b.emit("vector", lambda e: e.tensor_max(ta3, lg3[:, :, 0:4], lg3[:, :, 4:8]),
           waits=t_lg, incs=[(dv, 1)])
    vnop()
    b.emit("vector", lambda e: e.tensor_max(kd128[:, 0:32].rearrange("p (g e) -> p g e", e=2),
                                            ta3[:, :, 0:2], ta3[:, :, 2:4]),
           incs=[(dv, 1)])
    vnop()
    b.emit("vector", lambda e: e.tensor_max(m1a[:, :], kd128[:, 0:32:2], kd128[:, 1:32:2]),
           incs=[(dv, 1)])
    vnop()
    b.emit("vector", lambda e: e.tensor_sub(kd128[:, :], lg_sb[:, :], m1bc(m1a)),
           incs=[(dv, 1)])
    vnop()
    b.emit("vector", lambda e: e.tensor_tensor(km1a[:, :], kd128[:, :], zer128[:, :],
                                               ALU.is_equal),
           incs=[(dv, 1)])
    vnop()
    b.emit("vector", lambda e: e.tensor_scalar_mul(mska[:, :], km1a[:, :], -3e38),
           incs=[(dv, 1)])
    vnop()
    b.emit("vector", lambda e: e.tensor_add(mska[:, :], lg_sb[:, :], mska[:, :]),
           incs=[(dv, 1)])
    vnop()
    msk3 = mska[:, :].rearrange("p (g e) -> p g e", e=8)
    b.emit("vector", lambda e: e.tensor_max(ta3, msk3[:, :, 0:4], msk3[:, :, 4:8]),
           incs=[(dv, 1)])
    vnop()
    b.emit("vector", lambda e: e.tensor_max(kd128[:, 0:32].rearrange("p (g e) -> p g e", e=2),
                                            ta3[:, :, 0:2], ta3[:, :, 2:4]),
           incs=[(dv, 1)])
    vnop()
    b.emit("vector", lambda e: e.tensor_max(m2a[:, :], kd128[:, 0:32:2], kd128[:, 1:32:2]),
           incs=[(dv, 1)])
    vnop()
    b.emit("vector", lambda e: e.tensor_sub(d12a[:, :], m1a[:, :], m2a[:, :]),
           incs=[(dv, 1)])
    vnop()
    b.emit("vector", lambda e: e.tensor_sub(kd128[:, :], lg_sb[:, :], m1bc(m2a)),
           incs=[(dv, 1)])
    vnop()
    t_d = b.emit("vector", lambda e: e.tensor_tensor(km2a[:, :], kd128[:, :], zer128[:, :],
                                                     ALU.is_equal),
                 incs=[(dv, 1)])[0]
    tp1 = b.emit("scalar", lambda e: e.activation(p1a[:, :], d12a[:, :], AF.Sigmoid),
                 waits=[t_d], incs=[(ac, 1)])[0]
    tp2 = b.emit("scalar", lambda e: e.activation(p2a[:, :], d12a[:, :], AF.Sigmoid,
                                                  scale=-1.0),
                 incs=[(ac, 1)])[0]
    b.emit("vector", lambda e: e.tensor_mul(mska[:, :].rearrange("p (g e) -> p g e", e=8),
                                            km1a[:, :].rearrange("p (g e) -> p g e", e=8),
                                            m1bc(p1a)),
           waits=[tp1], incs=[(dv, 1)])
    b.emit("vector", lambda e: e.tensor_mul(kd128[:, :].rearrange("p (g e) -> p g e", e=8),
                                            km2a[:, :].rearrange("p (g e) -> p g e", e=8),
                                            m1bc(p2a)),
           waits=[tp2], incs=[(dv, 1)])
    vnop()
    tcw = b.emit("vector", lambda e: e.tensor_add(cwtok[:, :], mska[:, :], kd128[:, :]),
                 incs=[(dv, 1)])[0]
    for tt in range(NTT):
        bankt = ps[6]
        papt = bankt.ap[0:8, 0:128]
        bank_prod(bankt,
                  lambda e, o=papt, i=cwtok[:, tt * 8:(tt + 1) * 8]:
                  e.transpose(o, i, identf[:, :]),
                  waits=[tcw])
        t_cw_rows.append(
            bank_cons(bankt, "vector",
                      lambda e, o=csn[0:8, tt * 128:(tt + 1) * 128], i=papt:
                      e.tensor_copy(o, i)))

    if dbg and "xn2" in dbg:
        xn_tok = xn_tok + [dma(st, "sync", d["d_xn2"], xn_sb[:, :], waits=xn_tok)]
    if dbg and "cw" in dbg:
        t_cw_rows = t_cw_rows + [dma(st, "sync", d["d_cw"], cwtok[:, :],
                                     waits=t_cw_rows)]
    if dbg and "lg" in dbg:
        t_cw_rows = t_cw_rows + [dma(st, "sync", d["d_lg"], lg_sb[:, :],
                                     waits=t_cw_rows)]

    W1A, W3A, W2A = 0, 2048, 4096
    cwrep = sf[:, 0:2048]
    silu_b = [sf[:, 2048 + i * 256: 2048 + (i + 1) * 256].bitcast(BF16) for i in range(2)]
    ucw_b = [sf[:, 2560 + i * 256: 2560 + (i + 1) * 256].bitcast(BF16) for i in range(2)]

    w1_tok, w3_tok, w2_tok = {}, {}, {}
    t_wbufs = (pe, pe.n)
    w1_free = {0: t_wbufs, 1: t_wbufs}
    w3_free = {0: t_wbufs, 1: t_wbufs}
    w2_free = {0: t_wbufs, 1: t_wbufs}

    def pf_w1(widx, hc):
        par = hc % 2
        dst = w_sb[:, W1A + par * 1024: W1A + (par + 1) * 1024].rearrange(
            "p (kc n) -> p kc n", kc=KC)
        src = d["w1"][widx].rearrange("(kc p) hf -> p kc hf", p=128)[:, :, hc * 128:(hc + 1) * 128]
        w1_tok[(widx, hc)] = dma(wd_w1[par], "gpsimd", dst, src, waits=[w1_free[par]])

    def pf_w3(widx, hc):
        par = hc % 2
        dst = w_sb[:, W3A + par * 1024: W3A + (par + 1) * 1024].rearrange(
            "p (kc n) -> p kc n", kc=KC)
        src = d["w3"][widx].rearrange("(kc p) hf -> p kc hf", p=128)[:, :, hc * 128:(hc + 1) * 128]
        w3_tok[(widx, hc)] = dma(wd_w3[par], "gpsimd", dst, src, waits=[w3_free[par]])

    def pf_w2(widx, dc):
        par = dc % 2
        dst = w_sb[:, W2A + par * 2048: W2A + (par + 1) * 2048].rearrange(
            "p (hc n) -> p hc n", hc=16)
        src = d["w2"][widx].rearrange("(hc p) dm -> p hc dm", p=128)[:, :, dc * 128:(dc + 1) * 128]
        w2_tok[(widx, dc)] = dma(wd_w2[par], "gpsimd", dst, src, waits=[w2_free[par]])

    eo_mul_tok = [None, None]
    pf_w1(0, 0)
    pf_w3(0, 0)
    pf_w1(0, 1)
    pf_w3(0, 1)
    pf_w2(0, 0)
    pf_w2(0, 1)

    e_order = list(range(NE)) + [NE]
    for ei, e_ in enumerate(e_order):
        shared = (e_ == NE)
        widx = 8 if shared else e_
        nwidx = e_order[ei + 1] if ei + 1 < len(e_order) else None
        if not shared:
            if e_ == 0:
                cwsrc = csn[0:1, 0:2048]
                tok_cwsrc = list(t_cw_rows)
            else:
                tok_cp = dma(sw_cw, "sync", csn[0:1, 2048:4096], csn[e_:e_ + 1, 0:2048],
                             waits=[(pe, pe.n)] + t_cw_rows)
                cwsrc = csn[0:1, 2048:4096]
                tok_cwsrc = [tok_cp]
            for tsb in range(NTS):
                bank = ps[tsb % 2]
                bank_prod(bank,
                          lambda e, o=bank.ap[:, :],
                          r=cwsrc[0:1, tsb * 512:(tsb + 1) * 512]:
                          e.matmul(o, ones1b[:, :], r, start=True, stop=True),
                          waits=tok_cwsrc)
                bank_cons(bank, "vector",
                          lambda e, o=cwrep[:, tsb * 512:(tsb + 1) * 512], i=bank.ap[:, :]:
                          e.tensor_copy(o, i))
        t_cwrep = (dv, dv.n)

        for hc in range(16):
            par = hc % 2
            for tsp in range(2):
              for kc in range(KC):
                for tsx in (2 * tsp, 2 * tsp + 1):
                    bank = ps[4 * tsp + (tsx - 2 * tsp)]
                    col = kc * T + tsx * 512
                    bank_prod(bank,
                              lambda e, o=bank.ap[:, :],
                              l=w_sb[:, W1A + par * 1024 + kc * 128:
                                     W1A + par * 1024 + (kc + 1) * 128],
                              r=xn_sb[:, col:col + 512],
                              s0=(kc == 0), s1=(kc == KC - 1):
                              e.matmul(o, l, r, start=s0, stop=s1),
                              waits=[w1_tok[(widx, hc)]] + xn_tok,
                              last=(kc == KC - 1))
              for kc in range(KC):
                for tsx in (2 * tsp, 2 * tsp + 1):
                    bank = ps[4 * tsp + 2 + (tsx - 2 * tsp)]
                    col = kc * T + tsx * 512
                    bank_prod(bank,
                              lambda e, o=bank.ap[:, :],
                              l=w_sb[:, W3A + par * 1024 + kc * 128:
                                     W3A + par * 1024 + (kc + 1) * 128],
                              r=xn_sb[:, col:col + 512],
                              s0=(kc == 0), s1=(kc == KC - 1):
                              e.matmul(o, l, r, start=s0, stop=s1),
                              waits=[w3_tok[(widx, hc)]] + xn_tok,
                              last=(kc == KC - 1))
              for tsx in (2 * tsp, 2 * tsp + 1):
                i2 = tsx - 2 * tsp
                gb_, ub_ = ps[4 * tsp + i2], ps[4 * tsp + 2 + i2]
                sb_b, ub_b = silu_b[i2], ucw_b[i2]
                tsg = bank_cons(gb_, "scalar",
                                lambda e, o=sb_b, i=gb_.ap[:, :]: e.activation(o, i, AF.Silu),
                                waits=[eo_mul_tok[i2]], ch=ac)
                if shared:
                    tuc = bank_cons(ub_, "vector",
                                    lambda e, o=ub_b, i=ub_.ap[:, :]: e.tensor_copy(o, i))
                else:
                    tuc = bank_cons(ub_, "vector",
                                    lambda e, o=ub_b, i=ub_.ap[:, :],
                                    c=cwrep[:, tsx * 512:(tsx + 1) * 512]:
                                    e.tensor_mul(o, i, c),
                                    waits=[t_cwrep])
                eocol = hc * T + tsx * 512
                eo_mul_tok[i2] = b.emit("vector",
                       lambda e, o=big[:, eocol:eocol + 512], a=sb_b, bb=ub_b:
                       e.tensor_mul(o, a, bb),
                       waits=[tsg, tuc], incs=[(dv, 1)])[0]
            w1_free[par] = (pe, pe.n)
            w3_free[par] = (pe, pe.n)
            nxt = hc + 2
            if nxt < 16:
                pf_w1(widx, nxt)
                pf_w3(widx, nxt)
            elif nwidx is not None:
                pf_w1(nwidx, nxt - 16)
                pf_w3(nwidx, nxt - 16)
        t_eo = (dv, dv.n)

        for dc in range(KC):
            par = dc % 2
            bset = ps[4 * (dc % 2): 4 * (dc % 2) + 4]
            for hc in range(16):
                for tsx in range(NTS):
                    bank = bset[tsx]
                    eocol = hc * T + tsx * 512
                    bank_prod(bank,
                              lambda e, o=bank.ap[:, :],
                              l=w_sb[:, W2A + par * 2048 + hc * 128:
                                     W2A + par * 2048 + (hc + 1) * 128],
                              r=big[:, eocol:eocol + 512],
                              s0=(hc == 0), s1=(hc == 15):
                              e.matmul(o, l, r, start=s0, stop=s1),
                              waits=[w2_tok[(widx, dc)], t_eo],
                              last=(hc == 15))
            for tsx in range(NTS):
                bank = bset[tsx]
                xcol = dc * T + tsx * 512
                bank_cons(bank, "vector",
                          lambda e, o=x_sb[:, xcol:xcol + 512], i=bank.ap[:, :]:
                          e.tensor_add(o, o, i))
            w2_free[par] = (pe, pe.n)
            nxt = dc + 2
            if nxt < KC:
                pf_w2(widx, nxt)
            elif nwidx is not None:
                pf_w2(nwidx, nxt - KC)
    t_final = (dv, dv.n)

    out_tiles = d["out"].rearrange("(g p) m -> p g m", p=128)
    slot = 0
    for g4 in range(4):
        par = g4 % 2
        evac_toks = []
        for ti in range(4):
            tt = g4 * 4 + ti
            for kc in range(KC):
                bank = ps[slot % 2]
                sl = (slot // 2) % 4
                pap = bank.ap[:, sl * 128:(sl + 1) * 128]
                col = kc * T + tt * 128
                bank_prod(bank,
                          lambda e, o=pap, i=x_sb[:, col:col + 128]:
                          e.transpose(o, i, identf[:, :]),
                          waits=[t_final, stage_free[par]])
                evac_toks.append(
                    bank_cons(bank, "vector",
                              lambda e,
                              o=stage[par][:, ti * 1024 + kc * 128: ti * 1024 + (kc + 1) * 128],
                              i=pap: e.tensor_copy(o, i)))
                slot += 1
        tok_st = dma(st_out[par], "sync", out_tiles[:, g4 * 4:(g4 + 1) * 4, :],
                     stage[par].rearrange("p (g m) -> p g m", g=4),
                     waits=evac_toks)
        stage_free[par] = tok_st

    b.emit("sync", lambda e: e.nop(),
           waits=[(st_out[0], st_out[0].n), (st_out[1], st_out[1].n),
                  (st_dbg, st_dbg.n)])

    with nc.Block() as block:
        block.sync(lambda e: [f(e) for f in b.q["sync"]])
        block.gpsimd(lambda e: [f(e) for f in b.q["gpsimd"]])
        block.tensor(lambda e: [f(e) for f in b.q["tensor"]])
        block.vector(lambda e: [f(e) for f in b.q["vector"]])
        block.scalar(lambda e: [f(e) for f in b.q["scalar"]])

    return nc


_consts = None
_nc_cache = None


def kernel(L0_attn_x, L0_attn_w_aq, L0_attn_w_ak, L0_attn_w_av, L0_attn_w_ao,
           L0_ffn_router_w, L0_ffn_experts_bias, L0_ffn_w1, L0_ffn_w3, L0_ffn_w2,
           L0_ffn_sw1, L0_ffn_sw3, L0_ffn_sw2):
    global _consts, _nc_cache
    if _consts is None:
        _consts = host_consts()
    x = np.ascontiguousarray(np.asarray(L0_attn_x, dtype=np.float32))
    wq = np.ascontiguousarray(np.asarray(L0_attn_w_aq, dtype=np.float32).reshape(DM, DM))
    wk = np.ascontiguousarray(np.asarray(L0_attn_w_ak, dtype=np.float32).reshape(DM, DM))
    wv = np.ascontiguousarray(np.asarray(L0_attn_w_av, dtype=np.float32).reshape(DM, DM))
    wo = np.ascontiguousarray(np.asarray(L0_attn_w_ao, dtype=np.float32).reshape(DM, DM))
    rw = np.ascontiguousarray(np.asarray(L0_ffn_router_w, dtype=np.float32))
    eb = np.ascontiguousarray(np.asarray(L0_ffn_experts_bias, dtype=np.float32))
    w1 = np.ascontiguousarray(np.concatenate(
        [np.asarray(L0_ffn_w1, dtype=np.float32),
         np.asarray(L0_ffn_sw1, dtype=np.float32)], axis=0))
    w3 = np.ascontiguousarray(np.concatenate(
        [np.asarray(L0_ffn_w3, dtype=np.float32),
         np.asarray(L0_ffn_sw3, dtype=np.float32)], axis=0))
    w2 = np.ascontiguousarray(np.concatenate(
        [np.asarray(L0_ffn_w2, dtype=np.float32),
         np.asarray(L0_ffn_sw2, dtype=np.float32)], axis=0))

    if _nc_cache is None:
        _nc_cache = build_nc()
    nc = _nc_cache

    shared = {"wq": wq, "wk": wk, "wv": wv, "wo": wo, "rw": rw, "eb": eb,
              "w1": w1, "w3": w3, "w2": w2, **_consts}
    in_maps = [dict(shared, x=np.ascontiguousarray(x[bi])) for bi in range(8)]
    res = run_bass_kernel_spmd(nc, in_maps, core_ids=list(range(8)))
    return np.stack([np.asarray(r["out"], dtype=np.float32) for r in res.results], axis=0)
